# version 02
# baseline (speedup 1.0000x reference)
import contextlib
import numpy as np
import ml_dtypes
import concourse.bass as bass
import concourse.mybir as mybir
from concourse.bass_utils import run_bass_kernel_spmd

F32 = mybir.dt.float32
BF16 = mybir.dt.bfloat16
I8 = mybir.dt.int8
AF = mybir.ActivationFunctionType
ALU = mybir.AluOpType
AX = mybir.AxisListType
BFD = ml_dtypes.bfloat16

B, S, D = 2, 8192, 2048
H, DK, DV, KCV = 16, 128, 128, 4
NT = 8192
C = 128
NCH = NT // C
OWN = 2048
NOC = OWN // C
OWNCH0 = NCH - NOC
SUP = 1024
NSUP = NT // SUP
TSL = 512
NSL = SUP // TSL
NTIL = SUP // 128
INTER = 8192
EPS = 1e-6
LNEPS = 1e-5
NEGM = -1e30

ENGINES = ("sync", "tensor", "vector", "scalar", "gpsimd")
DEBUG_TAPS = ()


class Emit:
    def __init__(self):
        self.q = {e: [] for e in ENGINES}
        self.cnt = {}
        self.sems = {}

    def do(self, eng, method, *args, sem=None, k=1, **kw):
        import traceback
        site = traceback.extract_stack(limit=2)[0]
        tag = f"{method} @ {site.filename.split('/')[-1]}:{site.lineno}"

        def base(e, m=method, a=args, kw2=kw, tag=tag):
            try:
                return getattr(e, m)(*a, **kw2)
            except Exception as ex:
                raise RuntimeError(f"emit failed: {tag}: {ex}") from ex
        if sem is not None:
            self.cnt[sem] += k
            self.q[eng].append(
                lambda e, b=base, sn=sem, kk=k: b(e).then_inc(self.sems[sn], kk))
            return self.cnt[sem]
        self.q[eng].append(base)
        return None

    def vsync(self):
        self.cnt["SV"] += 1
        v = self.cnt["SV"]
        self.q["vector"].append(
            lambda e: e.engine_nop().then_inc(self.sems["SV"], 1))
        self.q["vector"].append(
            lambda e, vv=v: e.wait_ge(self.sems["SV"], vv))

    def wait(self, eng, sem, val=None):
        v = self.cnt[sem] if val is None else val
        if v > 0:
            self.q[eng].append(lambda e, sn=sem, vv=v: e.wait_ge(self.sems[sn], vv))

    def join(self):
        for eng in ENGINES:
            for s_ in ("SP", "SV", "SA", "SX", "SS", "SW", "SL", "SG"):
                self.wait(eng, s_)

    def flush(self, block):
        for eng in ENGINES:
            ops = self.q[eng]
            if not ops:
                continue

            def run(e, ops=ops):
                for f in ops:
                    f(e)
            getattr(block, eng)(run)
        self.q = {e: [] for e in ENGINES}


class Builder:
    def __init__(self, debug_taps=()):
        self.taps = set(debug_taps)
        self.nc = bass.Bass("TRN2", target_bir_lowering=False)
        self.em = Emit()
        self.dbg = {}

    def declare(self):
        nc = self.nc
        scrk = "ExternalOutput" if "scr" in self.taps else "Internal"
        self.xw = nc.dram_tensor("xw", [NT, D], F32, kind="ExternalInput").ap()
        self.wkv = nc.dram_tensor("wkv", [8, 16, 128, 512], F32, kind="ExternalInput").ap()
        self.wqi = nc.dram_tensor("wqi", [4, 16, 128, 512], F32, kind="ExternalInput").ap()
        self.wzi = nc.dram_tensor("wzi", [4, 16, 128, 512], F32, kind="ExternalInput").ap()
        self.wb_in = nc.dram_tensor("wb_in", [16, 128, 16], F32, kind="ExternalInput").ap()
        self.wa_in = nc.dram_tensor("wa_in", [16, 128, 16], F32, kind="ExternalInput").ap()
        self.cwi = nc.dram_tensor("cwi", [128, 48 * 4], F32, kind="ExternalInput").ap()
        self.adt = nc.dram_tensor("adt", [16, 2], F32, kind="ExternalInput").ap()
        self.nw_in = nc.dram_tensor("nw_in", [128, 128], F32, kind="ExternalInput").ap()
        self.wo = nc.dram_tensor("wo", [16, 128, 2048], F32, kind="ExternalInput").ap()
        self.wg8 = nc.dram_tensor("wg8", [64, 16, 128, 128], I8, kind="ExternalInput").ap()
        self.wu8 = nc.dram_tensor("wu8", [64, 16, 128, 128], I8, kind="ExternalInput").ap()
        self.wd8 = nc.dram_tensor("wd8", [64, 128, 2048], I8, kind="ExternalInput").ap()
        self.sgu = nc.dram_tensor("sgu", [128, 64 * 2], F32, kind="ExternalInput").ap()
        self.sdn = nc.dram_tensor("sdn", [1, 2048], F32, kind="ExternalInput").ap()
        self.identb = nc.dram_tensor("identb", [128, 128], BF16, kind="ExternalInput").ap()
        self.maskSi = nc.dram_tensor("maskSi", [128, 128], F32, kind="ExternalInput").ap()
        self.maskIi = nc.dram_tensor("maskIi", [128, 128], F32, kind="ExternalInput").ap()
        self.out = nc.dram_tensor("out", [OWN, D], F32, kind="ExternalOutput").ap()
        self.scr_k = nc.dram_tensor("scr_k", [NCH, 128, H * 128], BF16, kind=scrk).ap()
        self.scr_v = nc.dram_tensor("scr_v", [NCH, 128, H * 128], BF16, kind=scrk).ap()
        self.scr_q = nc.dram_tensor("scr_q", [NOC, 128, H * 128], BF16, kind=scrk).ap()
        self.scr_z = nc.dram_tensor("scr_z", [NOC, 128, H * 128], BF16, kind=scrk).ap()
        self.scr_xnT = nc.dram_tensor("scr_xnT", [16, 128, OWN], BF16, kind=scrk).ap()
        self.scr_g = nc.dram_tensor("scr_g", [NOC, 128, H * 128], BF16,
                                    kind="ExternalOutput" if "scr_g" in self.taps
                                    else "Internal").ap()
        self.scr_s = nc.dram_tensor("scr_s", [NOC, 128, 2048], BF16,
                                    kind="ExternalOutput" if "scr_s" in self.taps
                                    else "Internal").ap()
        if "rows" in self.taps:
            self.dbg_rows = nc.dram_tensor("dbg_rows", [128, 2 * NT], F32,
                                           kind="ExternalOutput").ap()
        if "ck" in self.taps:
            self.dbg_ck = nc.dram_tensor("dbg_ck", [6, 128, 2048], F32,
                                         kind="ExternalOutput").ap()

    def build(self, phases=("A", "S", "W", "F")):
        em = self.em
        nc = self.nc
        self.declare()
        with contextlib.ExitStack() as es0:
            self.es0 = es0
            sb0 = lambda n, s_, d: es0.enter_context(nc.sbuf_tensor(n, s_, d))
            self.identt = sb0("identt", [128, 128], BF16)
            self.identf = sb0("identf", [128, 128], F32)
            self.maskSt = sb0("maskSt", [128, 128], F32)
            self.maskIt = sb0("maskIt", [128, 128], F32)
            self.nwt = sb0("nwt", [128, 128], F32)
            self.adts = sb0("adts", [16, 2], F32)
            self.negea = sb0("negea", [128, 1], F32)
            self.onescol = sb0("onescol", [128, 1], BF16)
            self.cst = sb0("cst", [128, 4], F32)
            self.onesrow = sb0("onesrow", [1, 128], BF16)
            self.onesrowf = sb0("onesrowf", [1, 128], F32)
            self.onesr16 = sb0("onesr16", [16, C], F32)
            self.PA = es0.enter_context(nc.psum_tensor("PA", [128, 2048], F32))
            self.PB = es0.enter_context(nc.psum_tensor("PB", [128, 2048], F32))
            for s_ in ("SX", "SS", "SW", "SP", "SV", "SA", "SG", "SL", "SO"):
                em.sems[s_] = es0.enter_context(nc.semaphore(s_))
                em.cnt[s_] = 0
            block = es0.enter_context(nc.Block())
            self.block = block

            self.a0_constants()
            es_rows = contextlib.ExitStack()
            self.rowsA = es_rows.enter_context(
                nc.sbuf_tensor("rowsA", [128, NT], F32))
            self.rowsB = es_rows.enter_context(
                nc.sbuf_tensor("rowsB", [128, NT], BF16))
            if "A" in phases:
                with contextlib.ExitStack() as es1:
                    self.es1 = es1
                    self.a1_alloc()
                    for s in range(NSUP):
                        self.a1_super(s)
                    self.a2_rows()
                    em.join()
                    em.flush(block)
            if "S" in phases:
                with contextlib.ExitStack() as es2:
                    self.es2 = es2
                    self.a3_alloc()
                    self.a3_scan()
                    em.join()
                    em.flush(block)
            es_rows.close()
            if "W" in phases:
                with contextlib.ExitStack() as es3:
                    self.es3 = es3
                    self.b_wout()
                    em.join()
                    em.flush(block)
            if "F" in phases:
                with contextlib.ExitStack() as es4:
                    self.es4 = es4
                    self.c_ffn()
                    em.join()
                    em.flush(block)
            em.wait("sync", "SO")
            em.flush(block)
        return nc, self.dbg

    def a0_constants(self):
        em = self.em
        em.do("sync", "dma_start", out=self.identt[:], in_=self.identb, sem="SX", k=16)
        em.do("sync", "dma_start", out=self.maskSt[:], in_=self.maskSi, sem="SX", k=16)
        em.do("sync", "dma_start", out=self.maskIt[:], in_=self.maskIi, sem="SX", k=16)
        em.do("sync", "dma_start", out=self.nwt[:], in_=self.nw_in, sem="SX", k=16)
        em.do("sync", "dma_start", out=self.adts[:], in_=self.adt, sem="SX", k=16)
        self.A0_SX = em.cnt["SX"]
        em.wait("vector", "SX", self.A0_SX)
        em.do("vector", "tensor_copy", out=self.identf[:], in_=self.identt[:])
        em.do("vector", "memset", self.onescol[:], 1.0)
        em.do("vector", "memset", self.onesrow[:], 1.0)
        em.do("vector", "memset", self.onesrowf[:], 1.0)
        em.do("vector", "memset", self.onesr16[:], 1.0)
        em.do("vector", "memset", self.cst[:, 0:1], EPS)
        em.do("vector", "memset", self.cst[:, 1:2], float(DK) * EPS)
        em.do("vector", "memset", self.cst[:, 2:3], 1.0, sem="SV")
        em.wait("scalar", "SX", self.A0_SX)
        em.do("scalar", "activation", out=self.negea[0:16, :],
              in_=self.adts[:, 0:1], func=AF.Exp)
        em.do("scalar", "mul", out=self.negea[0:16, :], in_=self.negea[0:16, :],
              mul=-1.0)
        em.do("scalar", "copy", out=self.negea[64:80, :], in_=self.negea[0:16, :],
              sem="SA")

    def a1_alloc(self):
        sb1 = lambda n, s_, d: self.es1.enter_context(self.nc.sbuf_tensor(n, s_, d))
        self.wgrp = [sb1(f"wgrp{i}", [128, 16 * 512], BF16) for i in range(2)]
        self.xnt = [sb1("xnt0", [128, 16 * SUP], BF16)] * 2
        self.xtk = [sb1(f"xtk{i}", [128, D], F32) for i in range(2)]
        self.xnb = [sb1("xnb0", [128, D], BF16)] * 2
        self.lncol = sb1("lncol", [128, 48], F32)
        self.mxb = sb1("mxb", [128, 4 * TSL], BF16)
        self.tails = sb1("tails", [128, 48 * 3], BF16)
        self.cwc = sb1("cwc", [128, 48 * 4], F32)
        self.cwd = [sb1("cwd0", [128, 16 * 128], BF16)] * 2
        self.ksil = [sb1(f"ksil{i}", [128, TSL], F32) for i in range(2)]
        self.sqb = [sb1(f"sqb{i}", [128, TSL], BF16) for i in range(2)]
        self.l2r = sb1("l2r", [1, 2 * TSL], F32)
        self.l2rb = sb1("l2rb", [1, 2 * TSL], BF16)
        self.vsil = [sb1(f"vsil{i}", [128, TSL], BF16) for i in range(2)]
        self.kstg = [sb1(f"kstg{i}", [128, TSL], BF16) for i in range(2)]
        self.vstg = [sb1(f"vstg{i}", [128, TSL], BF16) for i in range(2)]
        self.bgrows = sb1("bgrows", [128, NT], F32)
        self.wbas = sb1("wbas", [128, 16 * 32], BF16)
        self.xtail = sb1("xtail", [128, 16 * 3], BF16)
        em = self.em
        em.do("sync", "dma_start", out=self.cwc[:], in_=self.cwi, sem="SX", k=16)
        self.CWC_SX = em.cnt["SX"]
        em.do("gpsimd", "dma_start",
              out=self.wbas[:].rearrange("p (d c) -> p d c", d=16)[:, :, 0:16],
              in_=self.wb_in.rearrange("d p c -> p d c"), sem="SW", k=16)
        em.do("gpsimd", "dma_start",
              out=self.wbas[:].rearrange("p (d c) -> p d c", d=16)[:, :, 16:32],
              in_=self.wa_in.rearrange("d p c -> p d c"), sem="SW", k=16)
        self.WBA_SW = em.cnt["SW"]
        em.wait("vector", "SX", self.CWC_SX)
        em.do("vector", "memset", self.tails[:], 0.0, sem="SV")
        self.TAILS_SV = em.cnt["SV"]
        self.x_loaded = {}
        self.ln_done = {}
        self.trp_done = {}
        self.trcp_done = {}
        self.trp_last = {}
        self.xnt_done = {}
        self.w_loaded = {}
        self.wl_free = {}
        self.proj_all = {-2: 0, -1: 0}
        self.slot_free = [0] * 4
        self.slot_conv = [0] * 4
        self.slot_evac = [0] * 4
        self.diag_used = [0, 0]
        self.diag_built = [0, 0]
        self.pb_cs_free = 0
        self.pb_bc_free = 0
        self.pb_tr_free = 0
        self.pb_ln_free = 0
        self.l2_free = [0, 0]
        self.ksil_free = [0, 0]
        self.sqb_free = [0, 0]
        self.vsil_free = [0, 0]
        self.kstg_free = [0, 0]
        self.vstg_free = [0, 0]
        self.tail_primed = False
        self.TAILS_SA = 0
        self.xnt_spill = 0
        self.WLIST = []
        for s_ in range(NSUP):
            self.WLIST += [(s_, k_, g_) for (k_, g_) in
                           ([("kv", i) for i in range(8)] +
                            ([("q", i) for i in range(4)] +
                             [("z", i) for i in range(4)] if s_ >= 6 else []))]
        self.wl_loaded_g = {}
        self.wl_free_g = {}
        self.wl_next = 0

    def load_w(self, gidx):
        em = self.em
        if gidx >= len(self.WLIST) or gidx < self.wl_next:
            return
        assert gidx == self.wl_next
        self.wl_next += 1
        (s_, kind, g) = self.WLIST[gidx]
        buf = self.wgrp[gidx % 2]
        if gidx >= 2:
            em.wait("gpsimd", "SA", self.wl_free_g[gidx - 2])
        src_ = {"kv": self.wkv, "q": self.wqi, "z": self.wzi}[kind]
        self.wl_loaded_g[gidx] = em.do(
            "gpsimd", "dma_start",
            out=buf[:].rearrange("p (d c) -> p d c", d=16),
            in_=src_[g].rearrange("d p c -> p d c"), sem="SW", k=16)

    def wloads_for(self, s):
        g = [("kv", i) for i in range(8)]
        if s >= 6:
            g += [("q", i) for i in range(4)] + [("z", i) for i in range(4)]
        return g

    def a1_super(self, s):
        em = self.em
        par = s % 2
        def ld_tile(gj):
            if gj >= NSUP * NTIL:
                return
            if gj >= 2:
                em.wait("sync", "SV", self.ln_done[gj - 2])
            self.x_loaded[gj] = em.do(
                "sync", "dma_start", out=self.xtk[gj % 2][:],
                in_=self.xw[gj * 128:(gj + 1) * 128, :], sem="SX", k=16)
        if s == 0:
            ld_tile(0)
            ld_tile(1)
        for tt in range(NTIL):
            gi = s * NTIL + tt
            p2 = (gi % 2) * 8
            lc = self.lncol
            em.wait("vector", "SX", self.x_loaded[gi])
            em.do("vector", "reduce_sum", out=lc[:, p2:p2 + 1],
                  in_=self.xtk[gi % 2][:], axis=AX.X, sem="SV")
            rsv = em.cnt["SV"]
            em.wait("scalar", "SX", self.x_loaded[gi])
            if gi >= 1:
                em.wait("scalar", "SP", self.trp_last[gi - 1])
            em.do("scalar", "activation", out=self.xnb[0][:],
                  in_=self.xtk[gi % 2][:], func=AF.Square,
                  accum_out=lc[:, p2 + 1:p2 + 2])
            em.do("scalar", "copy", out=lc[:, p2 + 7:p2 + 8],
                  in_=lc[:, p2 + 1:p2 + 2], sem="SA")
            sqv = em.cnt["SA"]
            em.wait("vector", "SA", sqv)
            em.vsync()
            em.do("vector", "tensor_scalar", out=lc[:, p2 + 2:p2 + 3],
                  in0=lc[:, p2:p2 + 1], scalar1=1.0 / D, scalar2=None,
                  op0=ALU.mult)
            em.do("vector", "tensor_scalar", out=lc[:, p2 + 3:p2 + 4],
                  in0=lc[:, p2 + 1:p2 + 2], scalar1=1.0 / D, scalar2=LNEPS,
                  op0=ALU.mult, op1=ALU.add)
            em.vsync()
            em.do("vector", "tensor_mul", out=lc[:, p2 + 4:p2 + 5],
                  in0=lc[:, p2 + 2:p2 + 3], in1=lc[:, p2 + 2:p2 + 3])
            em.vsync()
            em.do("vector", "tensor_sub", out=lc[:, p2 + 3:p2 + 4],
                  in0=lc[:, p2 + 3:p2 + 4], in1=lc[:, p2 + 4:p2 + 5],
                  sem="SV")
            em.wait("scalar", "SV")
            em.do("scalar", "activation", out=lc[:, p2 + 5:p2 + 6],
                  in_=lc[:, p2 + 3:p2 + 4], func=AF.Sqrt)
            em.do("scalar", "copy", out=lc[:, p2 + 7:p2 + 8],
                  in_=lc[:, p2 + 5:p2 + 6], sem="SA")
            em.wait("vector", "SA")
            em.do("vector", "reciprocal", out=lc[:, p2 + 5:p2 + 6],
                  in_=lc[:, p2 + 5:p2 + 6])
            em.vsync()
            em.do("vector", "tensor_scalar", out=lc[:, p2 + 6:p2 + 7],
                  in0=lc[:, p2 + 5:p2 + 6], scalar1=lc[:, p2 + 2:p2 + 3],
                  scalar2=-1.0, op0=ALU.mult, op1=ALU.mult)
            em.vsync()
            cmv = em.cnt["SV"]
            if gi >= 1:
                em.wait("vector", "SP", self.trp_last[gi - 1])
            em.wait("vector", "SA")
            self.ln_done[gi] = em.do(
                "vector", "tensor_scalar", out=self.xnb[0][:],
                in_=None, sem="SV") if False else None
            em.do("vector", "tensor_scalar", out=self.xnb[0][:],
                  in0=self.xtk[gi % 2][:], scalar1=lc[:, p2 + 2:p2 + 3],
                  scalar2=lc[:, p2 + 5:p2 + 6], op0=ALU.subtract, op1=ALU.mult,
                  sem="SV")
            self.ln_done[gi] = em.cnt["SV"]
            em.wait("tensor", "SV", self.ln_done[gi])
            if tt == 0:
                em.wait("tensor", "SA", self.proj_all[s - 1])
            for grp4 in range(4):
                em.wait("tensor", "SV", self.pb_ln_free)
                for d4 in range(4):
                    dchunk = grp4 * 4 + d4
                    em.do("tensor", "transpose",
                          out=self.PB[:, 1024:1280].bitcast(BF16)[:, d4 * 128:
                                                                  (d4 + 1) * 128],
                          in_=self.xnb[0][:, dchunk * 128:(dchunk + 1) * 128],
                          identity=self.identt[:],
                          sem=("SP" if d4 == 3 else None))
                self.trp_done[(gi, grp4)] = em.cnt["SP"]
                em.wait("vector", "SP", self.trp_done[(gi, grp4)])
                if grp4 == 0 and tt == 0:
                    em.wait("vector", "SA", self.proj_all[s - 1])
                    em.wait("vector", "SS", self.xnt_spill)
                for d4 in range(4):
                    dchunk = grp4 * 4 + d4
                    em.do("vector", "tensor_copy",
                          out=self.xnt[0][:, dchunk * SUP + tt * 128:
                                          dchunk * SUP + (tt + 1) * 128],
                          in_=self.PB[:, 1024:1280].bitcast(BF16)[:, d4 * 128:
                                                                  (d4 + 1) * 128],
                          sem=("SV" if d4 == 3 else None))
                self.pb_ln_free = em.cnt["SV"]
            self.trp_last[gi] = self.trp_done[(gi, 3)]
            ld_tile(gi + 2)
        self.xnt_done[s] = em.cnt["SV"]

        for sl in range(NSL):
            for woff, kindb in ((0, "b"), (16, "a")):
                em.wait("tensor", "SV", self.xnt_done[s])
                em.wait("tensor", "SW", self.WBA_SW)
                em.wait("tensor", "SA", self.slot_free[0])
                for dchunk in range(16):
                    em.do("tensor", "matmul", out=self.PA[0:16, 0:512],
                          lhsT=self.wbas[:, dchunk * 32 + woff:
                                         dchunk * 32 + woff + 16],
                          rhs=self.xnt[par][:, dchunk * SUP + sl * TSL:
                                            dchunk * SUP + (sl + 1) * TSL],
                          start=(dchunk == 0), stop=(dchunk == 15),
                          sem=("SP" if dchunk == 15 else None))
                pv = em.cnt["SP"]
                tok0 = s * SUP + sl * TSL
                em.wait("scalar", "SP", pv)
                if kindb == "b":
                    self.slot_free[0] = em.do(
                        "scalar", "activation",
                        out=self.bgrows[32:48, tok0:tok0 + TSL],
                        in_=self.PA[0:16, 0:512], func=AF.Sigmoid, sem="SA")
                else:
                    self.slot_free[0] = em.do(
                        "scalar", "copy", out=self.bgrows[0:16, tok0:tok0 + TSL],
                        in_=self.PA[0:16, 0:512], sem="SA")

        wl = self.wloads_for(s)
        gbase = sum(len(self.wloads_for(s2)) for s2 in range(s))
        if s == 0:
            self.load_w(0)
            self.load_w(1)
        for wi, (kind, g) in enumerate(wl):
            gidx = gbase + wi
            buf = self.wgrp[gidx % 2]
            dpar = gidx % 2
            if kind != "z":
                cc0 = {"kv": g * 4, "q": 32 + g * 4}[kind]
                em.wait("vector", "SP", self.diag_used[dpar])
                em.wait("vector", "SX", self.CWC_SX)
                for i4 in range(4):
                    for j in range(KCV):
                        cc = cc0 + i4
                        em.do("vector", "tensor_scalar",
                              out=self.cwd[dpar][:, (i4 * 4 + j) * 128:
                                                 (i4 * 4 + j + 1) * 128],
                              in0=self.identt[:],
                              scalar1=self.cwc[:, cc * 4 + j:cc * 4 + j + 1],
                              scalar2=None, op0=ALU.mult,
                              sem=("SV" if (i4, j) == (3, 3) else None))
                self.diag_built[dpar] = em.cnt["SV"]
            if kind == "q" and s == 6 and g == 0:
                pass
            if kind == "q" and s == 6:
                self.prime_q_tails(g, gidx)
            for sl in range(NSL):
                self.wload_slice(s, par, kind, g, gidx, sl, buf, dpar)
            self.wl_free_g[gidx] = max(self.slot_free)
            self.load_w(gidx + 2)
        self.proj_all[s] = max(self.slot_free)

        if s == 5:
            em.wait("vector", "SV", self.xnt_done[5])
            for d in range(16):
                em.do("vector", "tensor_copy",
                      out=self.xtail[:, d * 3:(d + 1) * 3],
                      in_=self.xnt[0][:, d * SUP + SUP - 3:d * SUP + SUP],
                      sem=("SV" if d == 15 else None))
            em.vsync()
            self.xtail_done = em.cnt["SV"]

        if s >= 6:
            em.wait("sync", "SV", self.xnt_done[s])
            em.do("sync", "dma_start",
                  out=self.scr_xnT[:, :, (s - 6) * SUP:(s - 5) * SUP].rearrange(
                      "d p t -> p d t"),
                  in_=self.xnt[par][:].rearrange("p (d t) -> p d t", d=16),
                  sem="SS", k=16)
            self.xnt_spill = em.cnt["SS"]

    def prime_q_tails(self, g, gidx):
        em = self.em
        buf = self.wgrp[gidx % 2]
        em.wait("tensor", "SW", self.wl_loaded_g[gidx])
        for i4 in range(4):
            cc = 32 + g * 4 + i4
            em.wait("tensor", "SV", self.TAILS_SV)
            em.wait("tensor", "SV", self.xtail_done)
            em.wait("tensor", "SA", self.pb_cs_free)
            for dchunk in range(16):
                em.do("tensor", "matmul", out=self.PB[:, 512:515],
                      lhsT=buf[:, dchunk * 512 + i4 * 128:
                               dchunk * 512 + (i4 + 1) * 128],
                      rhs=self.xtail[:, dchunk * 3:(dchunk + 1) * 3],
                      start=(dchunk == 0), stop=(dchunk == 15),
                      sem=("SP" if dchunk == 15 else None))
            pv = em.cnt["SP"]
            em.wait("scalar", "SP", pv)
            em.do("scalar", "copy", out=self.tails[:, cc * 3:cc * 3 + 3],
                  in_=self.PB[:, 512:515])
            self.pb_cs_free = em.do(
                "scalar", "copy", out=self.lncol[:, 40:41],
                in_=self.tails[:, cc * 3 + 2:cc * 3 + 3], sem="SA")
        self.TAILS_SA = self.pb_cs_free

    def wload_slice(self, s, par, kind, g, wi, sl, buf, dpar):
        em = self.em
        c0 = s * NTIL + sl * 4
        tok0 = sl * TSL
        for i4 in range(4):
            slot = i4
            pslc = self.PA[:, slot * 512:slot * 512 + 512]
            em.wait("tensor", "SW", self.wl_loaded_g[wi])
            em.wait("tensor", "SV", self.xnt_done[s])
            em.wait("tensor", "SA", self.slot_free[slot])
            for dchunk in range(16):
                em.do("tensor", "matmul", out=pslc,
                      lhsT=buf[:, dchunk * 512 + i4 * 128:
                               dchunk * 512 + (i4 + 1) * 128],
                      rhs=self.xnt[par][:, dchunk * SUP + tok0:
                                        dchunk * SUP + tok0 + TSL],
                      start=(dchunk == 0), stop=(dchunk == 15),
                      sem=("SP" if dchunk == 15 else None))
            projv = em.cnt["SP"]

            if kind == "z":
                h = g * 4 + i4
                vp = i4 % 2
                em.wait("scalar", "SP", projv)
                em.wait("scalar", "SP", self.vsil_free[vp])
                self.slot_free[slot] = em.do(
                    "scalar", "activation", out=self.vsil[vp][:], in_=pslc,
                    func=AF.Silu, sem="SA")
                self.vz_transpose_store(self.scr_z, c0 - OWNCH0, h, vp,
                                        self.slot_free[slot])
                continue

            em.wait("vector", "SP", projv)
            em.do("vector", "tensor_copy",
                  out=self.mxb[:, slot * TSL:(slot + 1) * TSL], in_=pslc, sem="SV")
            mev = em.cnt["SV"]
            self.slot_evac[slot] = mev

            cc = {"kv": g * 4 + i4 if g < 4 else 16 + (g - 4) * 4 + i4,
                  "q": 32 + g * 4 + i4}[kind]
            em.wait("tensor", "SV", mev)
            em.wait("tensor", "SV", self.diag_built[dpar])
            em.wait("tensor", "SV", self.TAILS_SV)
            em.wait("tensor", "SA", self.TAILS_SA)
            mms = []
            for j in range(KCV):
                mms.append((pslc[:, 3 - j:512],
                            self.cwd[dpar][:, (i4 * 4 + j) * 128:
                                           (i4 * 4 + j + 1) * 128],
                            self.mxb[:, slot * TSL:slot * TSL + 509 + j]))
                if j < 3:
                    mms.append((pslc[:, 0:3 - j],
                                self.cwd[dpar][:, (i4 * 4 + j) * 128:
                                               (i4 * 4 + j + 1) * 128],
                                self.tails[:, cc * 3 + j:cc * 3 + 3]))
            for mi, (o_, l_, r_) in enumerate(mms):
                em.do("tensor", "matmul", out=o_, lhsT=l_, rhs=r_,
                      start=(mi == 0), stop=(mi == len(mms) - 1),
                      sem=("SP" if mi == len(mms) - 1 else None))
            convv = em.cnt["SP"]
            self.slot_conv[slot] = convv
            self.diag_used[dpar] = convv
            em.wait("vector", "SP", convv)
            em.do("vector", "tensor_copy", out=self.tails[:, cc * 3:cc * 3 + 3],
                  in_=self.mxb[:, slot * TSL + 509:slot * TSL + 512], sem="SV")
            self.TAILS_SV = em.cnt["SV"]

            if kind == "kv" and g >= 4:
                h = (g - 4) * 4 + i4
                vp = i4 % 2
                em.wait("scalar", "SP", convv)
                em.wait("scalar", "SP", self.vsil_free[vp])
                self.slot_free[slot] = em.do(
                    "scalar", "activation", out=self.vsil[vp][:], in_=pslc,
                    func=AF.Silu, sem="SA")
                self.vz_transpose_store(self.scr_v, c0, h, vp,
                                        self.slot_free[slot])
            else:
                h = g * 4 + i4
                target = self.scr_k if kind == "kv" else self.scr_q
                cbase = c0 if kind == "kv" else c0 - OWNCH0
                self.kq_pipeline(slot, convv, h, i4 % 2, kind, target, cbase)

    def kq_pipeline(self, slot, convv, h, bp, kind, target, cbase):
        em = self.em
        pslc = self.PA[:, slot * 512:slot * 512 + 512]
        em.wait("scalar", "SP", convv)
        em.wait("scalar", "SV", self.ksil_free[bp])
        em.do("scalar", "activation", out=self.ksil[bp][:], in_=pslc,
              func=AF.Silu, sem="SA")
        em.wait("scalar", "SP", self.sqb_free[bp])
        self.slot_free[slot] = em.do(
            "scalar", "activation", out=self.sqb[bp][:], in_=self.ksil[bp][:],
            func=AF.Square, sem="SA")
        sqv = self.slot_free[slot]
        em.wait("tensor", "SA", sqv)
        em.wait("tensor", "SA", self.pb_cs_free)
        em.do("tensor", "matmul", out=self.PB[0:1, 512:1024], lhsT=self.onescol[:],
              rhs=self.sqb[bp][:], start=True, stop=True, sem="SP")
        csv = em.cnt["SP"]
        self.sqb_free[bp] = csv
        em.wait("scalar", "SP", csv)
        em.wait("scalar", "SV", self.l2_free[bp])
        scale = float(DK) if kind == "q" else 1.0
        epsc = self.cst[0:1, 1:2] if kind == "q" else self.cst[0:1, 0:1]
        em.do("scalar", "activation", out=self.l2r[0:1, bp * TSL:(bp + 1) * TSL],
              in_=self.PB[0:1, 512:1024], func=AF.Sqrt, scale=scale,
              bias=epsc)
        self.pb_cs_free = em.do(
            "scalar", "copy", out=self.l2r[0:1, bp * TSL + TSL - 1:
                                           bp * TSL + TSL],
            in_=self.l2r[0:1, bp * TSL + TSL - 1:bp * TSL + TSL], sem="SA")
        em.wait("vector", "SA", self.pb_cs_free)
        em.do("vector", "reciprocal", out=self.l2r[0:1, bp * TSL:(bp + 1) * TSL],
              in_=self.l2r[0:1, bp * TSL:(bp + 1) * TSL])
        em.vsync()
        em.do("vector", "tensor_copy", out=self.l2rb[0:1, bp * TSL:(bp + 1) * TSL],
              in_=self.l2r[0:1, bp * TSL:(bp + 1) * TSL], sem="SV")
        rbv = em.cnt["SV"]
        em.wait("tensor", "SV", rbv)
        em.wait("tensor", "SV", self.pb_bc_free)
        em.do("tensor", "matmul", out=self.PB[:, 0:512], lhsT=self.onesrow[:],
              rhs=self.l2rb[0:1, bp * TSL:(bp + 1) * TSL], start=True, stop=True,
              sem="SP")
        bcv = em.cnt["SP"]
        em.wait("vector", "SP", bcv)
        em.wait("vector", "SS", self.kstg_free[bp])
        em.do("vector", "tensor_mul", out=self.kstg[bp][:], in0=self.ksil[bp][:],
              in1=self.PB[:, 0:512], sem="SV")
        mv = em.cnt["SV"]
        self.pb_bc_free = mv
        self.l2_free[bp] = mv
        self.ksil_free[bp] = mv
        em.wait("sync", "SV", mv)
        self.kstg_free[bp] = em.do(
            "sync", "dma_start",
            out=target[cbase:cbase + 4, :, h * 128:(h + 1) * 128].rearrange(
                "c p t -> p c t"),
            in_=self.kstg[bp][:].rearrange("p (c t) -> p c t", c=4),
            sem="SS", k=16)

    def vz_transpose_store(self, target, cbase, h, vp, siluv):
        em = self.em
        em.wait("tensor", "SA", siluv)
        em.wait("tensor", "SV", self.pb_tr_free)
        for t4 in range(4):
            em.do("tensor", "transpose",
                  out=self.PB[:, 1536:1792].bitcast(BF16)[:, t4 * 128:(t4 + 1) * 128],
                  in_=self.vsil[vp][:, t4 * 128:(t4 + 1) * 128],
                  identity=self.identt[:], sem=("SP" if t4 == 3 else None))
        tv = em.cnt["SP"]
        self.vsil_free[vp] = tv
        em.wait("vector", "SP", tv)
        em.wait("vector", "SS", self.vstg_free[vp])
        em.do("vector", "tensor_copy", out=self.vstg[vp][:],
              in_=self.PB[:, 1536:1792].bitcast(BF16)[:, 0:512], sem="SV")
        cv = em.cnt["SV"]
        self.pb_tr_free = cv
        em.wait("sync", "SV", cv)
        self.vstg_free[vp] = em.do(
            "sync", "dma_start",
            out=target[cbase:cbase + 4, :, h * 128:(h + 1) * 128].rearrange(
                "c p d -> p c d"),
            in_=self.vstg[vp][:].rearrange("p (c d) -> p c d", c=4),
            sem="SS", k=16)

    def a2_rows(self):
        em = self.em
        bg = self.bgrows
        rA, rB = self.rowsA, self.rowsB
        em.wait("vector", "SA")
        em.wait("scalar", "SA")
        em.wait("scalar", "SV")
        em.do("vector", "tensor_scalar", out=bg[64:80, :], in0=bg[0:16, :],
              scalar1=self.adts[:, 1:2], scalar2=None, op0=ALU.add,
              sem="SV")
        em.wait("scalar", "SV")
        em.do("scalar", "activation", out=rA[64:80, :], in_=bg[64:80, :],
              func=AF.Abs)
        em.do("scalar", "activation", out=rA[64:80, :], in_=rA[64:80, :],
              func=AF.Exp, scale=-1.0)
        em.do("scalar", "activation", out=rA[64:80, :], in_=rA[64:80, :],
              func=AF.Ln, bias=self.cst[0:16, 2:3], sem="SA")
        em.wait("vector", "SA")
        em.do("vector", "tensor_scalar", out=bg[64:80, :], in0=bg[64:80, :],
              scalar1=0.0, scalar2=None, op0=ALU.max)
        em.vsync()
        em.do("vector", "tensor_add", out=bg[64:80, :], in0=bg[64:80, :],
              in1=rA[64:80, :])
        em.vsync()
        em.do("vector", "tensor_scalar", out=bg[0:16, :], in0=bg[64:80, :],
              scalar1=self.negea[64:80, :], scalar2=None, op0=ALU.mult,
              sem="SV")
        gv = em.cnt["SV"]
        em.wait("vector", "SV", gv)
        em.vsync()
        for c in range(NCH):
            em.do("vector", "tensor_tensor_scan",
                  out=rA[0:16, c * C:(c + 1) * C], data0=self.onesr16[:],
                  data1=bg[0:16, c * C:(c + 1) * C], initial=0.0,
                  op0=ALU.mult, op1=ALU.add,
                  sem=("SV" if c == NCH - 1 else None))
        Gv = em.cnt["SV"]
        em.wait("scalar", "SV", Gv)
        em.do("scalar", "activation", out=bg[0:16, :], in_=bg[32:48, :],
              func=AF.Ln, sem="SA")
        em.wait("vector", "SA")
        em.vsync()
        em.do("vector", "tensor_add", out=rA[32:48, :], in0=rA[0:16, :],
              in1=bg[0:16, :])
        em.do("vector", "tensor_copy", out=rB[0:16, :], in_=bg[32:48, :], sem="SV")
        for c in range(NCH):
            em.do("vector", "tensor_scalar", out=bg[64:80, c * C:(c + 1) * C],
                  in0=rA[0:16, c * C:(c + 1) * C],
                  scalar1=rA[0:16, c * C + C - 1:c * C + C],
                  scalar2=None, op0=ALU.subtract,
                  sem=("SV" if c == NCH - 1 else None))
        em.wait("scalar", "SV")
        em.do("scalar", "activation", out=rB[32:48, :], in_=bg[64:80, :],
              func=AF.Exp, scale=-1.0, sem="SA")
        if "rows" in self.taps:
            em.wait("sync", "SA")
            em.wait("sync", "SV")
            em.do("sync", "dma_start", out=self.dbg_rows[:, 0:NT], in_=rA[:],
                  sem="SS", k=16)
            em.wait("gpsimd", "SA")
            em.wait("gpsimd", "SV")
            em.do("gpsimd", "dma_start",
                  out=self.dbg_rows[0:64, NT:NT + NT], in_=rB[0:64, :],
                  sem="SS", k=16)

    def a3_alloc(self):
        sb2 = lambda n, s_, d: self.es2.enter_context(self.nc.sbuf_tensor(n, s_, d))
        self.Sst = sb2("Sst", [128, H * 128], F32)
        self.SbfT = sb2("SbfT", [128, H * 128], BF16)
        self.kT = [sb2(f"kT{i}", [128, 2048], BF16) for i in range(2)]
        self.vT = [sb2(f"vT{i}", [128, 2048], BF16) for i in range(2)]
        self.qT = [sb2(f"qT{i}", [128, 2048], BF16) for i in range(2)]
        self.zT = [sb2(f"zT{i}", [128, 2048], BF16) for i in range(2)]
        self.w0 = sb2("w0", [128, 2048], F32)
        self.w1 = sb2("w1", [128, 2048], F32)
        self.w2 = sb2("w2", [128, 2048], F32)
        self.uf = sb2("uf", [128, 2048], F32)
        self.bb = [sb2(f"bb{i}", [128, 2048], BF16) for i in range(9)]
        self.colsA = sb2("colsA", [128, 128], F32)
        self.prow = [sb2(f"prow{i}", [1, 4096], F32) for i in range(2)]
        self.gatedb = sb2("gatedb", [128, 2048], BF16)

    def a3_scan(self):
        em = self.em
        PA, PB = self.PA, self.PB
        em.join()
        em.do("vector", "memset", self.Sst[:], 0.0)
        em.do("vector", "memset", self.SbfT[:], 0.0, sem="SV")
        self.ld_ck = {}
        self.ck_done = {}
        self.ck_done_sv = {}
        self.pbq_free = 0
        self.d1_free = 0
        self.pbA_free = 0
        self.pa_last = 0
        self.pb_last = 0
        self.a3_load(0)
        for c in range(NCH):
            self.a3_chunk(c)

    def a3_load(self, c):
        if c >= NCH:
            return
        em = self.em
        par = c % 2
        key = c - 2
        if key >= 0:
            em.wait("scalar", "SP", self.ck_done.get(key, 0))
            em.wait("scalar", "SV", self.ck_done_sv.get(key, 0))
        n = em.do("scalar", "dma_start", out=self.kT[par][:], in_=self.scr_k[c],
                  sem="SL", k=16)
        n = em.do("scalar", "dma_start", out=self.vT[par][:], in_=self.scr_v[c],
                  sem="SL", k=16)
        if c >= OWNCH0:
            n = em.do("scalar", "dma_start", out=self.qT[par][:],
                      in_=self.scr_q[c - OWNCH0], sem="SL", k=16)
            n = em.do("scalar", "dma_start", out=self.zT[par][:],
                      in_=self.scr_z[c - OWNCH0], sem="SL", k=16)
        n = em.do("scalar", "dma_start",
                  out=self.prow[par][0:1, 0:2048],
                  in_=self.rowsA[32:48, c * C:(c + 1) * C], sem="SL", k=16)
        if c >= OWNCH0:
            n = em.do("scalar", "dma_start",
                      out=self.prow[par][0:1, 2048:4096],
                      in_=self.rowsA[0:16, c * C:(c + 1) * C], sem="SL", k=16)
        self.ld_ck[c] = em.cnt["SL"]

    def a3_chunk(self, c):
        em = self.em
        PA, PB = self.PA, self.PB
        own = c >= OWNCH0
        par = c % 2
        kT, vT, qT, zT = (self.kT[par], self.vT[par], self.qT[par], self.zT[par])
        cA = self.colsA
        self.a3_load(c + 1)

        def heads(ap, h):
            return ap[:, h * 128:(h + 1) * 128]

        def stretch(sl, nh=16, off=0):
            return cA[:, sl * 16 + off:sl * 16 + off + nh].unsqueeze(2)                 .broadcast_to([128, nh, 128])

        def wide(ap, nh=16, off=0):
            return ap[:, off * 128:(off + nh) * 128].rearrange(
                "p (h d) -> p h d", h=nh)

        if c == 0:
            em.wait("vector", "SL", self.ld_ck[0])
            em.do("vector", "tensor_copy", out=self.bb[7][:], in_=kT[:], sem="SV")
            em.do("vector", "tensor_copy", out=self.bb[7][:], in_=vT[:], sem="SV")
        em.wait("tensor", "SL", self.ld_ck[c])
        em.wait("tensor", "SV", self.pb_last)
        em.do("tensor", "transpose", out=PB[:, 1792:1808],
              in_=self.rowsA[0:16, c * C:(c + 1) * C],
              identity=self.identf[0:16, 0:16])
        em.do("tensor", "transpose",
              out=PB[:, 1808:1816].bitcast(BF16)[:, 0:16],
              in_=self.rowsB[0:16, c * C:(c + 1) * C],
              identity=self.identt[0:16, 0:16])
        em.do("tensor", "transpose",
              out=PB[:, 1816:1824].bitcast(BF16)[:, 0:16],
              in_=self.rowsB[32:48, c * C:(c + 1) * C],
              identity=self.identt[32:48, 32:48], sem="SP")
        trv = em.cnt["SP"]
        em.wait("vector", "SP", trv)
        em.do("vector", "tensor_copy", out=cA[:, 0:16], in_=PB[:, 1792:1808])
        em.do("vector", "tensor_copy", out=cA[:, 16:32],
              in_=PB[:, 1808:1816].bitcast(BF16)[:, 0:16])
        em.do("vector", "tensor_copy", out=cA[:, 32:48],
              in_=PB[:, 1816:1824].bitcast(BF16)[:, 0:16], sem="SV")
        self.pbq_free = em.cnt["SV"]
        em.vsync()
        em.wait("scalar", "SV")
        em.do("scalar", "activation", out=cA[:, 48:64], in_=cA[:, 0:16],
              func=AF.Exp)
        em.do("scalar", "copy", out=cA[:, 112:113], in_=cA[:, 63:64], sem="SA")
        em.wait("vector", "SA")
        em.do("vector", "tensor_mul", out=cA[:, 64:80], in0=cA[:, 16:32],
              in1=cA[:, 48:64])
        em.do("vector", "tensor_mul", out=cA[:, 80:96], in0=cA[:, 32:48],
              in1=cA[:, 48:64], sem="SV")
        em.vsync()
        colv = em.cnt["SV"]

        em.wait("tensor", "SL", self.ld_ck[c])
        em.wait("tensor", "SV", self.pa_last)
        for i in range(4):
            em.do("tensor", "matmul", out=PA[:, i * 512:(i + 1) * 512],
                  lhsT=self.onesrowf[:],
                  rhs=self.prow[par][0:1, i * 512:(i + 1) * 512],
                  start=True, stop=True, sem=("SP" if i == 3 else None))
        bv = em.cnt["SP"]
        em.wait("vector", "SP", bv)
        em.wait("vector", "SV", colv)
        em.do("vector", "scalar_tensor_tensor",
              out=self.w0[:].rearrange("p (h d) -> p h d", h=16),
              in0=stretch(0), scalar=-1.0,
              in1=PA[:].rearrange("p (h d) -> p h d", h=16),
              op0=ALU.mult, op1=ALU.add, sem="SV")
        d1bv = em.cnt["SV"]
        if own:
            em.wait("tensor", "SV", d1bv)
            for i in range(4):
                em.do("tensor", "matmul", out=PA[:, i * 512:(i + 1) * 512],
                      lhsT=self.onesrowf[:],
                      rhs=self.prow[par][0:1, 2048 + i * 512:2048 + (i + 1) * 512],
                      start=True, stop=True, sem=("SP" if i == 3 else None))
            bv2 = em.cnt["SP"]
            em.wait("vector", "SP", bv2)
            em.do("vector", "scalar_tensor_tensor",
                  out=self.w1[:].rearrange("p (h d) -> p h d", h=16),
                  in0=stretch(0), scalar=-1.0,
                  in1=PA[:].rearrange("p (h d) -> p h d", h=16),
                  op0=ALU.mult, op1=ALU.add, sem="SV")
        self.d1_free = em.cnt["SV"]

        em.wait("tensor", "SV", self.pbq_free)
        for h in range(H):
            em.do("tensor", "matmul", out=PB[:, h * 128:(h + 1) * 128],
                  lhsT=heads(kT, h), rhs=heads(kT, h), start=True, stop=True,
                  sem=("SP" if h == H - 1 else None))
        Av = em.cnt["SP"]
        em.do("vector", "tensor_add", out=self.w2[:].rearrange(
            "p (h d) -> p h d", h=16), in0=self.w0[:].rearrange(
            "p (h d) -> p h d", h=16),
            in1=self.maskSt[:].unsqueeze(1).broadcast_to([128, 16, 128]), sem="SV")
        em.wait("scalar", "SV")
        em.do("scalar", "activation", out=self.bb[0][:], in_=self.w2[:],
              func=AF.Exp, sem="SA")
        em.wait("vector", "SA")
        em.wait("vector", "SP", Av)
        em.do("vector", "scalar_tensor_tensor", out=self.bb[1][:],
              in0=self.bb[0][:], scalar=-1.0, in1=PB[:],
              op0=ALU.mult, op1=ALU.mult, sem="SV")
        ntv = em.cnt["SV"]
        if own:
            em.wait("tensor", "SV", ntv)
            for h in range(H):
                em.do("tensor", "matmul", out=PB[:, h * 128:(h + 1) * 128],
                      lhsT=heads(kT, h), rhs=heads(qT, h), start=True, stop=True,
                      sem=("SP" if h == H - 1 else None))
            KQv = em.cnt["SP"]
            em.do("vector", "tensor_add", out=self.w2[:].rearrange(
                "p (h d) -> p h d", h=16), in0=self.w1[:].rearrange(
                "p (h d) -> p h d", h=16),
                in1=self.maskIt[:].unsqueeze(1).broadcast_to([128, 16, 128]), sem="SV")
            em.wait("scalar", "SV")
            em.do("scalar", "activation", out=self.bb[0][:], in_=self.w2[:],
                  func=AF.Exp, sem="SA")
            em.wait("vector", "SA")
            em.wait("vector", "SP", KQv)
            em.do("vector", "tensor_mul", out=self.bb[2][:], in0=self.bb[0][:],
                  in1=PB[:], sem="SV")
        pbAv = em.cnt["SV"]
        self.pbA_free = pbAv

        em.wait("tensor", "SV", pbAv)
        for h in range(H):
            em.do("tensor", "transpose",
                  out=PA[:, 0:1024].bitcast(BF16)[:, h * 128:(h + 1) * 128],
                  in_=heads(self.bb[1], h), identity=self.identt[:],
                  sem=("SP" if h == H - 1 else None))
        Ntr = em.cnt["SP"]
        em.wait("vector", "SP", Ntr)
        em.do("vector", "tensor_copy", out=self.bb[3][:],
              in_=PA[:, 0:1024].bitcast(BF16)[:, 0:2048], sem="SV")
        Nv = em.cnt["SV"]

        em.wait("tensor", "SV", Nv)
        for h in range(H):
            em.do("tensor", "matmul", out=PB[:, h * 128:(h + 1) * 128],
                  lhsT=heads(self.bb[1], h), rhs=heads(self.bb[3], h),
                  start=True, stop=True, sem=("SP" if h == H - 1 else None))
        P2v = em.cnt["SP"]
        em.wait("scalar", "SP", P2v)
        em.do("scalar", "copy", out=self.bb[4][:], in_=PB[:], sem="SA")
        em.wait("tensor", "SA")
        for h in range(H):
            em.do("tensor", "matmul", out=PA[:, h * 128:(h + 1) * 128],
                  lhsT=heads(self.bb[3], h), rhs=heads(self.bb[1], h),
                  start=True, stop=True, sem=("SP" if h == H - 1 else None))
        P2Tv = em.cnt["SP"]
        em.wait("scalar", "SP", P2Tv)
        em.do("scalar", "copy", out=self.bb[5][:], in_=PA[:], sem="SA")
        em.do("vector", "tensor_add",
              out=self.bb[6][:].rearrange("p (h d) -> p h d", h=16),
              in0=self.bb[1][:].rearrange("p (h d) -> p h d", h=16),
              in1=self.identt[:].unsqueeze(1).broadcast_to([128, 16, 128]), sem="SV")
        em.vsync()
        MTv = em.cnt["SV"]
        em.wait("tensor", "SA")
        em.wait("tensor", "SV", MTv)
        for h in range(H):
            em.do("tensor", "matmul", out=PB[:, h * 128:(h + 1) * 128],
                  lhsT=heads(self.bb[4], h), rhs=heads(self.bb[6], h),
                  start=True, stop=True, sem=("SP" if h == H - 1 else None))
        u1v = em.cnt["SP"]
        em.wait("vector", "SP", u1v)
        em.do("vector", "tensor_add", out=self.bb[6][:], in0=self.bb[6][:],
              in1=PB[:], sem="SV")
        em.vsync()
        em.wait("tensor", "SA")
        for h in range(H):
            em.do("tensor", "matmul", out=PA[:, h * 128:(h + 1) * 128],
                  lhsT=heads(self.bb[5], h), rhs=heads(self.bb[4], h),
                  start=True, stop=True, sem=("SP" if h == H - 1 else None))
        P4v = em.cnt["SP"]
        em.wait("scalar", "SP", P4v)
        em.do("scalar", "copy", out=self.bb[5][:], in_=PA[:], sem="SA")
        em.wait("tensor", "SA")
        em.wait("tensor", "SV")
        for h in range(H):
            em.do("tensor", "matmul", out=PB[:, h * 128:(h + 1) * 128],
                  lhsT=heads(self.bb[5], h), rhs=heads(self.bb[6], h),
                  start=True, stop=True, sem=("SP" if h == H - 1 else None))
        u2v = em.cnt["SP"]
        em.wait("vector", "SP", u2v)
        em.do("vector", "tensor_add", out=self.bb[6][:], in0=self.bb[6][:],
              in1=PB[:], sem="SV")
        em.vsync()
        MTfin = em.cnt["SV"]

        em.wait("tensor", "SV", MTfin)
        for h in range(H):
            em.do("tensor", "matmul", out=PA[:, h * 128:(h + 1) * 128],
                  lhsT=heads(kT, h), rhs=heads(self.SbfT, h), start=True,
                  stop=True, sem=("SP" if h == H - 1 else None))
        KSv = em.cnt["SP"]
        em.wait("vector", "SP", KSv)
        em.do("vector", "tensor_tensor",
              out=self.uf[:].rearrange("p (h d) -> p h d", h=16),
              in0=PA[:].rearrange("p (h d) -> p h d", h=16),
              in1=stretch(3), op=ALU.mult, sem="SV")
        em.vsync()
        em.do("vector", "tensor_sub", out=self.uf[:], in0=vT[:], in1=self.uf[:],
              sem="SV")
        em.vsync()
        em.do("vector", "tensor_tensor",
              out=self.bb[8][:].rearrange("p (h d) -> p h d", h=16),
              in0=self.uf[:].rearrange("p (h d) -> p h d", h=16),
              in1=stretch(1), op=ALU.mult, sem="SV")
        ubv = em.cnt["SV"]

        em.wait("tensor", "SV", ubv)
        for h in range(H):
            em.do("tensor", "matmul", out=PB[:, h * 128:(h + 1) * 128],
                  lhsT=heads(self.bb[6], h), rhs=heads(self.bb[8], h),
                  start=True, stop=True, sem=("SP" if h == H - 1 else None))
        dv = em.cnt["SP"]
        em.wait("scalar", "SP", dv)
        em.do("scalar", "copy", out=self.bb[0][:], in_=PB[:], sem="SA")
        dbv = em.cnt["SA"]

        if own:
            em.wait("tensor", "SV", ubv)
            for h in range(H):
                em.do("tensor", "matmul", out=PA[:, h * 128:(h + 1) * 128],
                      lhsT=heads(qT, h), rhs=heads(self.SbfT, h), start=True,
                      stop=True, sem=("SP" if h == H - 1 else None))
            QSv = em.cnt["SP"]
            em.wait("vector", "SP", QSv)
            em.do("vector", "tensor_tensor",
                  out=self.w0[:].rearrange("p (h d) -> p h d", h=16),
                  in0=PA[:].rearrange("p (h d) -> p h d", h=16),
                  in1=stretch(3), op=ALU.mult, sem="SV")
            Ofv = em.cnt["SV"]
            em.wait("tensor", "SA", dbv)
            em.wait("tensor", "SV", Ofv)
            for h in range(H):
                em.do("tensor", "matmul", out=PB[:, h * 128:(h + 1) * 128],
                      lhsT=heads(self.bb[2], h), rhs=heads(self.bb[0], h),
                      start=True, stop=True, sem=("SP" if h == H - 1 else None))
            Adv = em.cnt["SP"]
            em.wait("vector", "SP", Adv)
            em.vsync()
            em.do("vector", "tensor_add", out=self.w0[:], in0=self.w0[:],
                  in1=PB[:], sem="SV")
            em.vsync()
            Ov = em.cnt["SV"]

        em.wait("tensor", "SA", dbv)
        em.wait("tensor", "SV")
        for h in range(H):
            em.do("tensor", "transpose",
                  out=PA[:, 0:1024].bitcast(BF16)[:, h * 128:(h + 1) * 128],
                  in_=heads(kT, h), identity=self.identt[:],
                  sem=("SP" if h == H - 1 else None))
        ktr = em.cnt["SP"]
        em.wait("vector", "SP", ktr)
        em.do("vector", "tensor_tensor",
              out=self.bb[1][:].rearrange("p (h d) -> p h d", h=16),
              in0=PA[:, 0:1024].bitcast(BF16)[:, 0:2048].rearrange(
                  "p (h d) -> p h d", h=16),
              in1=stretch(2), op=ALU.mult, sem="SV")
        Kpv = em.cnt["SV"]

        em.wait("tensor", "SV", Kpv)
        for h in range(H):
            em.do("tensor", "matmul", out=PB[:, h * 128:(h + 1) * 128],
                  lhsT=heads(self.bb[1], h), rhs=heads(self.bb[0], h),
                  start=True, stop=True, sem=("SP" if h == H - 1 else None))
        KpDv = em.cnt["SP"]
        em.do("vector", "tensor_tensor",
              out=self.Sst[:].rearrange("p (h d) -> p h d", h=16),
              in0=self.Sst[:].rearrange("p (h d) -> p h d", h=16),
              in1=stretch(5), op=ALU.mult, sem="SV")
        em.vsync()
        em.wait("vector", "SP", KpDv)
        em.do("vector", "tensor_add", out=self.Sst[:], in0=self.Sst[:],
              in1=PB[:], sem="SV")
        em.vsync()
        em.do("vector", "tensor_copy", out=self.SbfT[:], in_=self.Sst[:],
              sem="SV")
        em.vsync()

        if own:
            em.wait("scalar", "SV", Ov)
            em.do("scalar", "activation", out=self.w1[:], in_=self.w0[:],
                  func=AF.Square, sem="SA")
            em.wait("vector", "SA")
            em.do("vector", "tensor_reduce",
                  out=cA[:, 96:112].unsqueeze(2),
                  in_=self.w1[:].rearrange("p (h d) -> p h d", h=16),
                  axis=AX.X, op=ALU.add, sem="SV")
            em.vsync()
            em.wait("scalar", "SV")
            em.do("scalar", "activation", out=cA[:, 96:112], in_=cA[:, 96:112],
                  func=AF.Sqrt, scale=1.0 / 128.0, bias=self.cst[:, 0:1])
            em.do("scalar", "copy", out=cA[:, 112:113], in_=cA[:, 111:112],
                  sem="SA")
            em.wait("vector", "SA")
            em.do("vector", "reciprocal", out=cA[:, 96:112], in_=cA[:, 96:112])
            em.vsync()
            em.do("vector", "tensor_tensor",
                  out=self.w1[:].rearrange("p (h d) -> p h d", h=16),
                  in0=self.w0[:].rearrange("p (h d) -> p h d", h=16),
                  in1=stretch(6), op=ALU.mult, sem="SV")
            em.vsync()
            em.do("vector", "tensor_tensor",
                  out=self.w1[:].rearrange("p (h d) -> p h d", h=16),
                  in0=self.w1[:].rearrange("p (h d) -> p h d", h=16),
                  in1=self.nwt[:].unsqueeze(1).broadcast_to([128, 16, 128]),
                  op=ALU.mult)
            em.vsync()
            em.wait("vector", "SS", getattr(self, "gspill", 0))
            em.do("vector", "tensor_mul", out=self.gatedb[:], in0=self.w1[:],
                  in1=zT[:], sem="SV")
            gv_ = em.cnt["SV"]
            em.wait("sync", "SV", gv_)
            self.gspill = em.do("sync", "dma_start", out=self.scr_g[c - OWNCH0],
                                in_=self.gatedb[:], sem="SS", k=16)
        if "ck" in self.taps and c == 48:
            em.wait("vector", "SA")
            em.wait("vector", "SV")
            em.vsync()
            for i, t_ in enumerate((self.bb[6], self.bb[0], None, self.bb[2],
                                    self.bb[1], self.bb[8])):
                if t_ is None:
                    continue
                em.do("vector", "tensor_copy", out=self.w2[:], in_=t_[:],
                      sem="SV")
                em.vsync()
                em.wait("sync", "SV")
                em.do("sync", "dma_start", out=self.dbg_ck[i], in_=self.w2[:],
                      sem="SS", k=16)
                em.wait("vector", "SS")
            em.do("sync", "dma_start", out=self.dbg_ck[2], in_=self.w0[:],
                  sem="SS", k=16)
            em.wait("vector", "SS")
        self.ck_done[c] = em.cnt["SP"]
        self.ck_done_sv[c] = em.cnt["SV"]
        self.pa_last = em.cnt["SV"]
        self.pb_last = em.cnt["SV"]

    def b_wout(self):
        em = self.em
        PA, PB = self.PA, self.PB
        sb3 = lambda n, s_, d: self.es3.enter_context(self.nc.sbuf_tensor(n, s_, d))
        wosb = sb3("wosb", [128, 16 * 2048], BF16)
        gld = [sb3(f"gld{i}", [128, 2048], BF16) for i in range(2)]
        gld2 = sb3("gld2", [128, 2048], BF16)
        gtT = sb3("gtT", [128, 2048], BF16)
        ssb = [sb3(f"ssb{i}", [128, 2048], BF16) for i in range(2)]
        em.do("gpsimd", "dma_start",
              out=wosb[:].rearrange("p (r c) -> p r c", r=16),
              in_=self.wo.rearrange("r p c -> p r c"), sem="SW", k=16)
        em.wait("vector", "SW")
        for i in range(16):
            em.do("vector", "tensor_copy", out=gld2[:],
                  in_=wosb[:, i * 2048:(i + 1) * 2048],
                  sem=("SV" if i == 15 else None))
        wov = em.cnt["SV"]
        g_ld = {}
        s_done = {}
        pa_tr_free = 0
        gtT_free = 0
        gld2_free = 0
        for cp in range(NOC):
            par = cp % 2
            if cp == 0 or cp == 1:
                pass
            if cp == 0:
                g_ld[0] = em.do("scalar", "dma_start", out=gld[0][:],
                                in_=self.scr_g[0], sem="SL", k=16)
            if cp + 1 < NOC:
                if cp - 1 >= 0:
                    em.wait("scalar", "SP", s_done.get(cp - 1, 0))
                g_ld[cp + 1] = em.do("scalar", "dma_start", out=gld[(cp + 1) % 2][:],
                                     in_=self.scr_g[cp + 1], sem="SL", k=16)
            em.wait("vector", "SL", g_ld[cp])
            em.wait("vector", "SP", gld2_free)
            em.do("vector", "tensor_copy", out=gld2[:], in_=gld[par][:], sem="SV")
            g2v = em.cnt["SV"]
            em.wait("tensor", "SV", g2v)
            em.wait("tensor", "SV", pa_tr_free)
            em.wait("tensor", "SP", gtT_free)
            for h in range(H):
                em.do("tensor", "transpose",
                      out=PA[:, 0:1024].bitcast(BF16)[:, h * 128:(h + 1) * 128],
                      in_=gld2[:, h * 128:(h + 1) * 128],
                      identity=self.identt[:], sem=("SP" if h == H - 1 else None))
            trv = em.cnt["SP"]
            gld2_free = trv
            em.wait("vector", "SP", trv)
            em.wait("vector", "SP", gtT_free)
            em.do("vector", "tensor_copy", out=gtT[:],
                  in_=PA[:, 0:1024].bitcast(BF16)[:, 0:2048], sem="SV")
            gtv = em.cnt["SV"]
            pa_tr_free = gtv
            em.wait("tensor", "SV", gtv)
            em.wait("tensor", "SV", wov)
            em.wait("tensor", "SV", s_done.get(("ev", cp - 1), 0))
            for col in range(4):
                for h in range(H):
                    em.do("tensor", "matmul",
                          out=PB[:, col * 512:(col + 1) * 512],
                          lhsT=gtT[:, h * 128:(h + 1) * 128],
                          rhs=wosb[:, h * 2048 + col * 512:h * 2048 + (col + 1) * 512],
                          start=(h == 0), stop=(h == H - 1),
                          sem=("SP" if h == H - 1 else None))
            mmv = em.cnt["SP"]
            s_done[cp] = mmv
            gtT_free = mmv
            em.wait("vector", "SP", mmv)
            em.wait("vector", "SS", s_done.get(("sp", cp - 2), 0))
            em.do("vector", "tensor_copy", out=ssb[par][:], in_=PB[:], sem="SV")
            sev = em.cnt["SV"]
            s_done[("ev", cp)] = sev
            em.wait("sync", "SV", sev)
            s_done[("sp", cp)] = em.do("sync", "dma_start", out=self.scr_s[cp],
                                       in_=ssb[par][:], sem="SS", k=16)

    def c_ffn(self):
        em = self.em
        PA, PB = self.PA, self.PB
        sb4 = lambda n, s_, d: self.es4.enter_context(self.nc.sbuf_tensor(n, s_, d))
        NICB = 8
        NSUPC = 64 // NICB
        QT = 512
        NQ = OWN // QT
        NTC = QT // 128
        xns = sb4("xns", [128, 16 * 2048], BF16)
        acc = sb4("acc", [128, NTC * 2048], F32)
        fuT = sb4("fuT", [128, NICB * QT], BF16)
        wgb = [sb4(f"wgb{i}", [128, 16 * 128], BF16) for i in range(2)]
        wub = [sb4(f"wub{i}", [128, 16 * 128], BF16) for i in range(2)]
        wdall = sb4("wdall", [128, NICB * 2048], BF16)
        sgus = sb4("sgus", [128, 128], F32)
        sdns = sb4("sdns", [1, 2048], F32)
        sdnb = sb4("sdnb", [128, 2048], F32)
        fgat = sb4("fgat", [128, QT], F32)
        xtl = sb4("xtl", [128, 2048], F32)
        stl = sb4("stl", [128, 2048], BF16)
        otl = sb4("otl", [128, 2048], F32)
        em.do("sync", "dma_start", out=sgus[:], in_=self.sgu, sem="SX", k=16)
        em.do("sync", "dma_start", out=sdns[:], in_=self.sdn, sem="SX", k=16)
        em.do("sync", "dma_start",
              out=xns[:].rearrange("p (d t) -> p d t", d=16),
              in_=self.scr_xnT.rearrange("d p t -> p d t"), sem="SX", k=16)
        ldv = em.cnt["SX"]
        em.wait("tensor", "SX", ldv)
        for i in range(4):
            em.do("tensor", "matmul", out=PA[:, i * 512:(i + 1) * 512],
                  lhsT=self.onesrowf[:], rhs=sdns[0:1, i * 512:(i + 1) * 512],
                  start=True, stop=True, sem=("SP" if i == 3 else None))
        em.wait("vector", "SP")
        em.do("vector", "tensor_copy", out=sdnb[:], in_=PA[:], sem="SV")
        sdv = em.cnt["SV"]

        wld = {}
        wl_used = {}
        wl_next = [0]

        def load_gu(j):
            if j >= NQ * 64 or j < wl_next[0]:
                return
            wl_next[0] += 1
            ic = j % 64
            par = j % 2
            if j >= 2:
                em.wait("gpsimd", "SP", wl_used.get(j - 2, 0))
            em.do("gpsimd", "dma_start",
                  out=wgb[par][:].rearrange("p (d c) -> p d c", d=16),
                  in_=self.wg8[ic].rearrange("d p c -> p d c"), sem="SW", k=16)
            wld[j] = em.do("gpsimd", "dma_start",
                           out=wub[par][:].rearrange("p (d c) -> p d c", d=16),
                           in_=self.wu8[ic].rearrange("d p c -> p d c"),
                           sem="SW", k=16)

        load_gu(0)
        load_gu(1)
        acc_ev = {}
        out_done = {}
        stl_read = 0
        fu_free_sp = 0
        fgat_free = 0
        fgat_sa = 0
        wd_free = 0
        for q in range(NQ):
            th0 = q * QT
            for p in range(NSUPC):
                em.wait("gpsimd", "SP", wd_free)
                for icb in range(NICB):
                    ic = p * NICB + icb
                    em.do("gpsimd", "dma_start",
                          out=wdall[:, icb * 2048:(icb + 1) * 2048],
                          in_=self.wd8[ic], sem="SW", k=16)
                wdv = em.cnt["SW"]
                for icb in range(NICB):
                    j = q * 64 + p * NICB + icb
                    ic = p * NICB + icb
                    par = j % 2
                    if j < 2:
                        em.wait("vector", "SW", wld[j])
                        em.do("vector", "tensor_copy", out=stl[:],
                              in_=wgb[par][:], sem="SV")
                        em.do("vector", "tensor_copy", out=stl[:],
                              in_=wub[par][:], sem="SV")
                        em.wait("tensor", "SV")
                    em.wait("tensor", "SW", wld[j])
                    em.wait("tensor", "SX", ldv)
                    em.wait("tensor", "SP", fu_free_sp)
                    em.wait("tensor", "SA", fgat_sa)
                    em.wait("tensor", "SV", fgat_free)
                    for up in range(2):
                        wsb = wub[par] if up else wgb[par]
                        pslc = PB[:, up * 512:(up + 1) * 512]
                        for d in range(16):
                            em.do("tensor", "matmul", out=pslc,
                                  lhsT=wsb[:, d * 128:(d + 1) * 128],
                                  rhs=xns[:, d * 2048 + th0:d * 2048 + th0 + QT],
                                  start=(d == 0), stop=(d == 15),
                                  sem=("SP" if (d, up) == (15, 1) else None))
                    guv = em.cnt["SP"]
                    wl_used[j] = guv
                    em.wait("scalar", "SP", guv)
                    em.wait("scalar", "SV", fgat_free)
                    em.do("scalar", "activation", out=fgat[:], in_=PB[:, 0:512],
                          func=AF.Silu, scale=sgus[:, ic * 2:ic * 2 + 1],
                          sem="SA")
                    fgv = em.cnt["SA"]
                    fgat_sa = fgv
                    em.wait("vector", "SA", fgv)
                    em.wait("vector", "SP", fu_free_sp)
                    em.do("vector", "scalar_tensor_tensor",
                          out=fuT[:, icb * QT:(icb + 1) * QT],
                          in0=PB[:, 512:1024],
                          scalar=sgus[:, ic * 2 + 1:ic * 2 + 2],
                          in1=fgat[:], op0=ALU.mult, op1=ALU.mult, sem="SV")
                    fgat_free = em.cnt["SV"]
                    load_gu(j + 2)
                fuv = em.cnt["SV"]
                em.wait("tensor", "SV", fuv)
                em.wait("tensor", "SW", wdv)
                for tc in range(NTC):
                    prev = acc_ev.get((q, p, tc - 1),
                                      acc_ev.get((q, p - 1, NTC - 1),
                                                 acc_ev.get((q - 1, NSUPC - 1,
                                                             NTC - 1), sdv)))
                    em.wait("tensor", "SV", prev)
                    for col in range(4):
                        for icb in range(NICB):
                            em.do("tensor", "matmul",
                                  out=PA[:, col * 512:(col + 1) * 512],
                                  lhsT=fuT[:, icb * QT + tc * 128:
                                           icb * QT + (tc + 1) * 128],
                                  rhs=wdall[:, icb * 2048 + col * 512:
                                            icb * 2048 + (col + 1) * 512],
                                  start=(icb == 0), stop=(icb == NICB - 1),
                                  sem=("SP" if (icb, col) == (NICB - 1, 3)
                                       else None))
                    dnv = em.cnt["SP"]
                    em.wait("vector", "SP", dnv)
                    if p == 0:
                        em.wait("vector", "SA")
                        em.do("vector", "tensor_copy",
                              out=acc[:, tc * 2048:(tc + 1) * 2048], in_=PA[:],
                              sem="SV")
                    else:
                        em.do("vector", "tensor_add",
                              out=acc[:, tc * 2048:(tc + 1) * 2048],
                              in0=acc[:, tc * 2048:(tc + 1) * 2048], in1=PA[:],
                              sem="SV")
                    acc_ev[(q, p, tc)] = em.cnt["SV"]
                fu_free_sp = em.cnt["SP"]
                wd_free = em.cnt["SP"]
            for tc in range(NTC):
                gi = q * NTC + tc
                tok0 = 6144 + q * QT + tc * 128
                em.wait("sync", "SO", out_done.get(gi - 1, 0))
                xv = em.do("sync", "dma_start", out=xtl[:],
                           in_=self.xw[tok0:tok0 + 128, :], sem="SX", k=16)
                em.wait("scalar", "SV", stl_read)
                sv_ = em.do("scalar", "dma_start", out=stl[:],
                            in_=self.scr_s[gi], sem="SL", k=16)
                em.wait("vector", "SV", acc_ev[(q, NSUPC - 1, tc)])
                em.wait("vector", "SV", sdv)
                em.wait("vector", "SO", out_done.get(gi - 1, 0))
                em.vsync()
                em.do("vector", "tensor_mul",
                      out=acc[:, tc * 2048:(tc + 1) * 2048],
                      in0=acc[:, tc * 2048:(tc + 1) * 2048], in1=sdnb[:],
                      sem="SV")
                em.vsync()
                em.wait("scalar", "SV")
                em.wait("scalar", "SO", out_done.get(gi - 1, 0))
                em.do("scalar", "activation", out=otl[:],
                      in_=acc[:, tc * 2048:(tc + 1) * 2048], func=AF.Silu,
                      sem="SA")
                gv2 = em.cnt["SA"]
                em.wait("vector", "SA", gv2)
                em.wait("vector", "SL", sv_)
                em.do("vector", "tensor_mul", out=otl[:], in0=otl[:],
                      in1=stl[:], sem="SV")
                stl_read = em.cnt["SV"]
                em.vsync()
                em.wait("vector", "SX", xv)
                em.do("vector", "tensor_add", out=otl[:], in0=otl[:],
                      in1=xtl[:], sem="SV")
                em.vsync()
                ov = em.cnt["SV"]
                em.wait("sync", "SV", ov)
                out_done[gi] = em.do(
                    "sync", "dma_start",
                    out=self.out[q * QT + tc * 128:q * QT + (tc + 1) * 128, :],
                    in_=otl[:], sem="SO", k=16)


def _prep_shared(inputs):
    w = {}
    Wq = np.ascontiguousarray(inputs["L0_attn_W_qkv"], np.float32)
    kv = Wq[:, 2048:6144]
    w["wkv"] = np.ascontiguousarray(
        kv.reshape(16, 128, 8, 512).transpose(2, 0, 1, 3))
    qq = Wq[:, 0:2048]
    w["wqi"] = np.ascontiguousarray(
        qq.reshape(16, 128, 4, 512).transpose(2, 0, 1, 3))
    Wz = np.ascontiguousarray(inputs["L0_attn_W_z"], np.float32)
    w["wzi"] = np.ascontiguousarray(
        Wz.reshape(16, 128, 4, 512).transpose(2, 0, 1, 3))
    w["wb_in"] = np.ascontiguousarray(
        inputs["L0_attn_W_b"].astype(np.float32).reshape(16, 128, 16))
    w["wa_in"] = np.ascontiguousarray(
        inputs["L0_attn_W_a"].astype(np.float32).reshape(16, 128, 16))
    cw = np.asarray(inputs["L0_attn_conv_w"], np.float32)[:, 0, :]
    reorder = np.concatenate([cw[2048:4096], cw[4096:6144], cw[0:2048]], 0)
    w["cwi"] = np.ascontiguousarray(
        reorder.reshape(48, 128, 4).transpose(1, 0, 2).reshape(128, 192))
    w["adt"] = np.ascontiguousarray(
        np.stack([inputs["L0_attn_A_log"], inputs["L0_attn_dt_bias"]], 1)
        .astype(np.float32))
    w["nw_in"] = np.ascontiguousarray(
        np.broadcast_to(inputs["L0_attn_norm_w"].astype(np.float32)[None, :],
                        (128, 128)))
    w["wo"] = np.ascontiguousarray(
        inputs["L0_attn_W_out"].astype(np.float32).reshape(16, 128, 2048))
    w["wg8"] = np.ascontiguousarray(
        inputs["L0_ffn_w_gate"].reshape(16, 128, 64, 128).transpose(2, 0, 1, 3))
    w["wu8"] = np.ascontiguousarray(
        inputs["L0_ffn_w_up"].reshape(16, 128, 64, 128).transpose(2, 0, 1, 3))
    w["wd8"] = np.ascontiguousarray(
        inputs["L0_ffn_w_down"].reshape(64, 128, 2048))
    sg = inputs["L0_ffn_s_gate"].astype(np.float32).reshape(64, 128)
    su = inputs["L0_ffn_s_up"].astype(np.float32).reshape(64, 128)
    w["sgu"] = np.ascontiguousarray(
        np.stack([sg, su], -1).transpose(1, 0, 2).reshape(128, 128))
    w["sdn"] = np.ascontiguousarray(
        inputs["L0_ffn_s_down"].astype(np.float32)[None, :])
    w["identb"] = np.eye(128, dtype=BFD)
    idx = np.arange(128)
    low_s = idx[:, None] > idx[None, :]
    w["maskSi"] = np.where(idx[:, None] < idx[None, :], 0.0, NEGM
                           ).astype(np.float32)
    w["maskIi"] = np.where(idx[:, None] <= idx[None, :], 0.0, NEGM
                           ).astype(np.float32)
    return w


def _make_window(x, b, j):
    end = (j + 1) * OWN
    start = end - NT
    if start >= 0:
        return np.ascontiguousarray(x[b, start:end])
    pad = np.zeros((-start, D), np.float32)
    return np.ascontiguousarray(np.concatenate([pad, x[b, 0:end]], 0))


def kernel(**inputs):
    shared = _prep_shared(inputs)
    x = np.asarray(inputs["L0_attn_hidden_states"], np.float32)
    nc, _ = Builder(DEBUG_TAPS).build()
    in_maps = []
    for core in range(8):
        b, j = core // 4, core % 4
        m = dict(shared)
        m["xw"] = _make_window(x, b, j)
        in_maps.append(m)
    res = run_bass_kernel_spmd(nc, in_maps, core_ids=list(range(8)))
    outp = np.empty((B, S, D), np.float32)
    for core in range(8):
        b, j = core // 4, core % 4
        outp[b, j * OWN:(j + 1) * OWN] = res.results[core]["out"]
    return outp


# version 04 (final)
# speedup vs baseline: 1.3842x; 1.3842x over previous
import contextlib
import numpy as np
import ml_dtypes
import concourse.bass as bass
import concourse.mybir as mybir
from concourse.bass_utils import run_bass_kernel_spmd

F32 = mybir.dt.float32
BF16 = mybir.dt.bfloat16
I8 = mybir.dt.int8
AF = mybir.ActivationFunctionType
ALU = mybir.AluOpType
AX = mybir.AxisListType
BFD = ml_dtypes.bfloat16

B, S, D = 2, 8192, 2048
H, DK, DV, KCV = 16, 128, 128, 4
NT = 8192
C = 128
NCH = NT // C
OWN = 2048
NOC = OWN // C
OWNCH0 = NCH - NOC
SUP = 1024
NSUP = NT // SUP
TSL = 512
NSL = SUP // TSL
NTIL = SUP // 128
INTER = 8192
EPS = 1e-6
LNEPS = 1e-5
NEGM = -1e30

ENGINES = ("sync", "tensor", "vector", "scalar", "gpsimd")
DEBUG_TAPS = ()


class Emit:
    def __init__(self):
        self.q = {e: [] for e in ENGINES}
        self.cnt = {}
        self.sems = {}

    def do(self, eng, method, *args, sem=None, k=1, **kw):
        import traceback
        site = traceback.extract_stack(limit=2)[0]
        tag = f"{method} @ {site.filename.split('/')[-1]}:{site.lineno}"

        def base(e, m=method, a=args, kw2=kw, tag=tag):
            try:
                return getattr(e, m)(*a, **kw2)
            except Exception as ex:
                raise RuntimeError(f"emit failed: {tag}: {ex}") from ex
        if sem is not None:
            self.cnt[sem] += k
            self.q[eng].append(
                lambda e, b=base, sn=sem, kk=k: b(e).then_inc(self.sems[sn], kk))
            return self.cnt[sem]
        self.q[eng].append(base)
        return None

    def vsync(self):
        self.cnt["SV"] += 1
        v = self.cnt["SV"]
        self.q["vector"].append(
            lambda e: e.engine_nop().then_inc(self.sems["SV"], 1))
        self.q["vector"].append(
            lambda e, vv=v: e.wait_ge(self.sems["SV"], vv))

    def wait(self, eng, sem, val=None):
        v = self.cnt[sem] if val is None else val
        if v > 0:
            self.q[eng].append(lambda e, sn=sem, vv=v: e.wait_ge(self.sems[sn], vv))

    def join(self):
        for eng in ENGINES:
            for s_ in ("SP", "SV", "SA", "SX", "SS", "SW", "SL", "SG"):
                self.wait(eng, s_)

    def flush(self, block):
        for eng in ENGINES:
            ops = self.q[eng]
            if not ops:
                continue

            def run(e, ops=ops):
                for f in ops:
                    f(e)
            getattr(block, eng)(run)
        self.q = {e: [] for e in ENGINES}


class Builder:
    def __init__(self, debug_taps=()):
        self.taps = set(debug_taps)
        self.nc = bass.Bass("TRN2", target_bir_lowering=False)
        self.em = Emit()
        self.dbg = {}

    def declare(self):
        nc = self.nc
        scrk = "ExternalOutput" if "scr" in self.taps else "Internal"
        self.xw = nc.dram_tensor("xw", [NT, D], F32, kind="ExternalInput").ap()
        self.wkv = nc.dram_tensor("wkv", [8, 16, 128, 512], F32, kind="ExternalInput").ap()
        self.wqi = nc.dram_tensor("wqi", [4, 16, 128, 512], F32, kind="ExternalInput").ap()
        self.wzi = nc.dram_tensor("wzi", [4, 16, 128, 512], F32, kind="ExternalInput").ap()
        self.wb_in = nc.dram_tensor("wb_in", [16, 128, 16], F32, kind="ExternalInput").ap()
        self.wa_in = nc.dram_tensor("wa_in", [16, 128, 16], F32, kind="ExternalInput").ap()
        self.cwi = nc.dram_tensor("cwi", [128, 48 * 4], F32, kind="ExternalInput").ap()
        self.adt = nc.dram_tensor("adt", [16, 2], F32, kind="ExternalInput").ap()
        self.nw_in = nc.dram_tensor("nw_in", [128, 128], F32, kind="ExternalInput").ap()
        self.wo = nc.dram_tensor("wo", [16, 128, 2048], F32, kind="ExternalInput").ap()
        self.wg8 = nc.dram_tensor("wg8", [64, 16, 128, 128], I8, kind="ExternalInput").ap()
        self.wu8 = nc.dram_tensor("wu8", [64, 16, 128, 128], I8, kind="ExternalInput").ap()
        self.wd8 = nc.dram_tensor("wd8", [64, 128, 2048], I8, kind="ExternalInput").ap()
        self.sgu = nc.dram_tensor("sgu", [128, 64 * 2], F32, kind="ExternalInput").ap()
        self.sdn = nc.dram_tensor("sdn", [1, 2048], F32, kind="ExternalInput").ap()
        self.identb = nc.dram_tensor("identb", [128, 128], BF16, kind="ExternalInput").ap()
        self.maskSi = nc.dram_tensor("maskSi", [128, 128], F32, kind="ExternalInput").ap()
        self.maskIi = nc.dram_tensor("maskIi", [128, 128], F32, kind="ExternalInput").ap()
        self.out = nc.dram_tensor("out", [OWN, D], F32, kind="ExternalOutput").ap()
        self.scr_k = nc.dram_tensor("scr_k", [NCH, 128, H * 128], BF16, kind=scrk).ap()
        self.scr_v = nc.dram_tensor("scr_v", [NCH, 128, H * 128], BF16, kind=scrk).ap()
        self.scr_q = nc.dram_tensor("scr_q", [NOC, 128, H * 128], BF16, kind=scrk).ap()
        self.scr_z = nc.dram_tensor("scr_z", [NOC, 128, H * 128], BF16, kind=scrk).ap()
        self.scr_xnT = nc.dram_tensor("scr_xnT", [16, 128, OWN], BF16, kind=scrk).ap()
        self.scr_g = nc.dram_tensor("scr_g", [NOC, 128, H * 128], BF16,
                                    kind="ExternalOutput" if "scr_g" in self.taps
                                    else "Internal").ap()
        self.scr_s = nc.dram_tensor("scr_s", [NOC, 128, 2048], BF16,
                                    kind="ExternalOutput" if "scr_s" in self.taps
                                    else "Internal").ap()
        if "rows" in self.taps:
            self.dbg_rows = nc.dram_tensor("dbg_rows", [128, 2 * NT], F32,
                                           kind="ExternalOutput").ap()
        if "ck" in self.taps:
            self.dbg_ck = nc.dram_tensor("dbg_ck", [6, 128, 2048], F32,
                                         kind="ExternalOutput").ap()

    def build(self, phases=("A", "S", "W", "F")):
        em = self.em
        nc = self.nc
        self.declare()
        with contextlib.ExitStack() as es0:
            self.es0 = es0
            sb0 = lambda n, s_, d: es0.enter_context(nc.sbuf_tensor(n, s_, d))
            self.identt = sb0("identt", [128, 128], BF16)
            self.identf = sb0("identf", [128, 128], F32)
            self.maskSt = sb0("maskSt", [128, 128], F32)
            self.maskIt = sb0("maskIt", [128, 128], F32)
            self.nwt = sb0("nwt", [128, 128], F32)
            self.adts = sb0("adts", [16, 2], F32)
            self.negea = sb0("negea", [128, 1], F32)
            self.onescol = sb0("onescol", [128, 1], BF16)
            self.cst = sb0("cst", [128, 4], F32)
            self.onesrow = sb0("onesrow", [1, 128], BF16)
            self.onesb = sb0("onesb", [128, 128], BF16)
            self.onesrowf = sb0("onesrowf", [1, 128], F32)
            self.onesr16 = sb0("onesr16", [16, C], F32)
            self.PA = es0.enter_context(nc.psum_tensor("PA", [128, 2048], F32))
            self.PB = es0.enter_context(nc.psum_tensor("PB", [128, 2048], F32))
            for s_ in ("SX", "SS", "SW", "SP", "SV", "SA", "SG", "SL", "SO"):
                em.sems[s_] = es0.enter_context(nc.semaphore(s_))
                em.cnt[s_] = 0
            block = es0.enter_context(nc.Block())
            self.block = block

            self.a0_constants()
            es_rows = contextlib.ExitStack()
            self.rowsA = es_rows.enter_context(
                nc.sbuf_tensor("rowsA", [128, NT], F32))
            self.rowsB = es_rows.enter_context(
                nc.sbuf_tensor("rowsB", [128, NT], BF16))
            if "A" in phases:
                with contextlib.ExitStack() as es1:
                    self.es1 = es1
                    self.a1_alloc()
                    for s in range(NSUP):
                        self.a1_super(s)
                    self.a2_rows()
                    em.join()
                    em.flush(block)
            if "S" in phases:
                with contextlib.ExitStack() as es2:
                    self.es2 = es2
                    self.a3_alloc()
                    self.a3_scan()
                    em.join()
                    em.flush(block)
            es_rows.close()
            if "W" in phases:
                with contextlib.ExitStack() as es3:
                    self.es3 = es3
                    self.b_wout()
                    em.join()
                    em.flush(block)
            if "F" in phases:
                with contextlib.ExitStack() as es4:
                    self.es4 = es4
                    self.c_ffn()
                    em.join()
                    em.flush(block)
            em.wait("sync", "SO")
            em.flush(block)
        return nc, self.dbg

    def a0_constants(self):
        em = self.em
        em.do("sync", "dma_start", out=self.identt[:], in_=self.identb, sem="SX", k=16)
        em.do("sync", "dma_start", out=self.maskSt[:], in_=self.maskSi, sem="SX", k=16)
        em.do("sync", "dma_start", out=self.maskIt[:], in_=self.maskIi, sem="SX", k=16)
        em.do("sync", "dma_start", out=self.nwt[:], in_=self.nw_in, sem="SX", k=16)
        em.do("sync", "dma_start", out=self.adts[:], in_=self.adt, sem="SX", k=16)
        self.A0_SX = em.cnt["SX"]
        em.wait("vector", "SX", self.A0_SX)
        em.do("vector", "tensor_copy", out=self.identf[:], in_=self.identt[:])
        em.do("vector", "memset", self.onescol[:], 1.0)
        em.do("vector", "memset", self.onesrow[:], 1.0)
        em.do("vector", "memset", self.onesb[:], 1.0)
        em.do("vector", "memset", self.onesrowf[:], 1.0)
        em.do("vector", "memset", self.onesr16[:], 1.0)
        em.do("vector", "memset", self.cst[:, 0:1], EPS)
        em.do("vector", "memset", self.cst[:, 1:2], float(DK) * EPS)
        em.do("vector", "memset", self.cst[:, 2:3], 1.0, sem="SV")
        em.wait("scalar", "SX", self.A0_SX)
        em.do("scalar", "activation", out=self.negea[0:16, :],
              in_=self.adts[:, 0:1], func=AF.Exp)
        em.do("scalar", "mul", out=self.negea[0:16, :], in_=self.negea[0:16, :],
              mul=-1.0)
        em.do("scalar", "copy", out=self.negea[64:80, :], in_=self.negea[0:16, :],
              sem="SA")

    def a1_alloc(self):
        sb1 = lambda n, s_, d: self.es1.enter_context(self.nc.sbuf_tensor(n, s_, d))
        self.wgrp = [sb1(f"wgrp{i}", [128, 16 * 512], BF16) for i in range(2)]
        self.xnt = [sb1("xnt0", [128, 16 * SUP], BF16)] * 2
        self.xtk = [sb1(f"xtk{i}", [128, D], F32) for i in range(2)]
        self.xnb = [sb1("xnb0", [128, D], BF16)] * 2
        self.lncol = sb1("lncol", [128, 48], F32)
        self.mxb = sb1("mxb", [128, 4 * TSL], BF16)
        self.tails = sb1("tails", [128, 48 * 3], BF16)
        self.cwc = sb1("cwc", [128, 48 * 4], F32)
        self.cwd = [sb1(f"cwd{i}", [128, 16 * 128], BF16)
                    for i in range(2)]
        self.ksil = [sb1(f"ksil{i}", [128, TSL], F32) for i in range(3)]
        self.sqb = [sb1(f"sqb{i}", [128, TSL], BF16) for i in range(2)]
        self.l2s = [sb1(f"l2s{i}", [128, TSL], F32) for i in range(2)]
        self.vsil = [sb1(f"vsil{i}", [128, TSL], BF16) for i in range(2)]
        self.kstg = [sb1(f"kstg{i}", [128, TSL], BF16) for i in range(3)]
        self.vstg = [sb1(f"vstg{i}", [128, TSL], BF16) for i in range(3)]
        self.bgrows = sb1("bgrows", [128, NT], F32)
        self.wbas = sb1("wbas", [128, 16 * 32], BF16)
        self.xtail = sb1("xtail", [128, 16 * 3], BF16)
        em = self.em
        em.do("sync", "dma_start", out=self.cwc[:], in_=self.cwi, sem="SX", k=16)
        self.CWC_SX = em.cnt["SX"]
        em.do("gpsimd", "dma_start",
              out=self.wbas[:].rearrange("p (d c) -> p d c", d=16)[:, :, 0:16],
              in_=self.wb_in.rearrange("d p c -> p d c"), sem="SW", k=16)
        em.do("gpsimd", "dma_start",
              out=self.wbas[:].rearrange("p (d c) -> p d c", d=16)[:, :, 16:32],
              in_=self.wa_in.rearrange("d p c -> p d c"), sem="SW", k=16)
        self.WBA_SW = em.cnt["SW"]
        em.wait("vector", "SX", self.CWC_SX)
        em.do("vector", "memset", self.tails[:], 0.0, sem="SV")
        self.TAILS_SV = em.cnt["SV"]
        self.x_loaded = {}
        self.ln_done = {}
        self.trp_done = {}
        self.trcp_done = {}
        self.trp_last = {}
        self.xnt_done = {}
        self.w_loaded = {}
        self.wl_free = {}
        self.proj_all = {-2: 0, -1: 0}
        self.slot_free = [0] * 4
        self.slot_evsv = [0] * 4
        self.conv_free = [0, 0]
        self.conv_tog = 0
        self.pb_l2_sa = 0
        self.diag_used = [0, 0]
        self.diag_built = [0, 0]
        self.pb_tr_free = 0
        self.pb_ln_free = 0
        self.l2s_free = [0, 0]
        self.ksil_free = [0, 0, 0]
        self.sqb_free = [0, 0]
        self.vsil_free = [0, 0]
        self.kstg_free = [0, 0, 0]
        self.vstg_free = [0, 0, 0]
        self.ucnt = 0
        self.ccnt = 0
        self.kqcnt = 0
        self.vzcnt = 0
        self.tail_primed = False
        self.TAILS_SA = 0
        self.xnt_spill = 0
        self.WLIST = []
        for s_ in range(NSUP):
            self.WLIST += [(s_, k_, g_) for (k_, g_) in
                           ([("kv", i) for i in range(8)] +
                            ([("q", i) for i in range(4)] +
                             [("z", i) for i in range(4)] if s_ >= 6 else []))]
        self.wl_loaded_g = {}
        self.wl_free_g = {}
        self.wl_next = 0

    def load_w(self, gidx):
        em = self.em
        if gidx >= len(self.WLIST) or gidx < self.wl_next:
            return
        assert gidx == self.wl_next
        self.wl_next += 1
        (s_, kind, g) = self.WLIST[gidx]
        buf = self.wgrp[gidx % 2]
        if gidx >= 2:
            em.wait("gpsimd", "SA", self.wl_free_g[gidx - 2])
        src_ = {"kv": self.wkv, "q": self.wqi, "z": self.wzi}[kind]
        self.wl_loaded_g[gidx] = em.do(
            "gpsimd", "dma_start",
            out=buf[:].rearrange("p (d c) -> p d c", d=16),
            in_=src_[g].rearrange("d p c -> p d c"), sem="SW", k=16)

    def wloads_for(self, s):
        g = [("kv", i) for i in range(8)]
        if s >= 6:
            g += [("q", i) for i in range(4)] + [("z", i) for i in range(4)]
        return g

    def a1_super(self, s):
        em = self.em
        par = s % 2
        def ld_tile(gj):
            if gj >= NSUP * NTIL:
                return
            if gj >= 2:
                em.wait("sync", "SV", self.ln_done[gj - 2])
            self.x_loaded[gj] = em.do(
                "sync", "dma_start", out=self.xtk[gj % 2][:],
                in_=self.xw[gj * 128:(gj + 1) * 128, :], sem="SX", k=16)
        if s == 0:
            ld_tile(0)
            ld_tile(1)
        for tt in range(NTIL):
            gi = s * NTIL + tt
            p2 = (gi % 2) * 8
            lc = self.lncol
            em.wait("vector", "SX", self.x_loaded[gi])
            em.do("vector", "reduce_sum", out=lc[:, p2:p2 + 1],
                  in_=self.xtk[gi % 2][:], axis=AX.X, sem="SV")
            rsv = em.cnt["SV"]
            em.wait("scalar", "SX", self.x_loaded[gi])
            if gi >= 1:
                em.wait("scalar", "SP", self.trp_last[gi - 1])
            em.do("scalar", "activation", out=self.xnb[0][:],
                  in_=self.xtk[gi % 2][:], func=AF.Square,
                  accum_out=lc[:, p2 + 1:p2 + 2])
            em.do("scalar", "copy", out=lc[:, p2 + 7:p2 + 8],
                  in_=lc[:, p2 + 1:p2 + 2], sem="SA")
            sqv = em.cnt["SA"]
            em.wait("vector", "SA", sqv)
            em.vsync()
            em.do("vector", "tensor_scalar", out=lc[:, p2 + 2:p2 + 3],
                  in0=lc[:, p2:p2 + 1], scalar1=1.0 / D, scalar2=None,
                  op0=ALU.mult)
            em.do("vector", "tensor_scalar", out=lc[:, p2 + 3:p2 + 4],
                  in0=lc[:, p2 + 1:p2 + 2], scalar1=1.0 / D, scalar2=LNEPS,
                  op0=ALU.mult, op1=ALU.add)
            em.vsync()
            em.do("vector", "tensor_mul", out=lc[:, p2 + 4:p2 + 5],
                  in0=lc[:, p2 + 2:p2 + 3], in1=lc[:, p2 + 2:p2 + 3])
            em.vsync()
            em.do("vector", "tensor_sub", out=lc[:, p2 + 3:p2 + 4],
                  in0=lc[:, p2 + 3:p2 + 4], in1=lc[:, p2 + 4:p2 + 5],
                  sem="SV")
            em.wait("scalar", "SV")
            em.do("scalar", "activation", out=lc[:, p2 + 5:p2 + 6],
                  in_=lc[:, p2 + 3:p2 + 4], func=AF.Sqrt)
            em.do("scalar", "copy", out=lc[:, p2 + 7:p2 + 8],
                  in_=lc[:, p2 + 5:p2 + 6], sem="SA")
            em.wait("vector", "SA")
            em.do("vector", "reciprocal", out=lc[:, p2 + 5:p2 + 6],
                  in_=lc[:, p2 + 5:p2 + 6])
            em.vsync()
            em.do("vector", "tensor_scalar", out=lc[:, p2 + 6:p2 + 7],
                  in0=lc[:, p2 + 5:p2 + 6], scalar1=lc[:, p2 + 2:p2 + 3],
                  scalar2=-1.0, op0=ALU.mult, op1=ALU.mult)
            em.vsync()
            cmv = em.cnt["SV"]
            if gi >= 1:
                em.wait("vector", "SP", self.trp_last[gi - 1])
            em.wait("vector", "SA")
            self.ln_done[gi] = em.do(
                "vector", "tensor_scalar", out=self.xnb[0][:],
                in_=None, sem="SV") if False else None
            em.do("vector", "tensor_scalar", out=self.xnb[0][:],
                  in0=self.xtk[gi % 2][:], scalar1=lc[:, p2 + 2:p2 + 3],
                  scalar2=lc[:, p2 + 5:p2 + 6], op0=ALU.subtract, op1=ALU.mult,
                  sem="SV")
            self.ln_done[gi] = em.cnt["SV"]
            em.wait("tensor", "SV", self.ln_done[gi])
            if tt == 0:
                em.wait("tensor", "SA", self.proj_all[s - 1])
            for grp4 in range(4):
                em.wait("tensor", "SV", self.pb_ln_free)
                for d4 in range(4):
                    dchunk = grp4 * 4 + d4
                    em.do("tensor", "transpose",
                          out=self.PB[:, 1024:1280].bitcast(BF16)[:, d4 * 128:
                                                                  (d4 + 1) * 128],
                          in_=self.xnb[0][:, dchunk * 128:(dchunk + 1) * 128],
                          identity=self.identt[:],
                          sem=("SP" if d4 == 3 else None))
                self.trp_done[(gi, grp4)] = em.cnt["SP"]
                em.wait("vector", "SP", self.trp_done[(gi, grp4)])
                if grp4 == 0 and tt == 0:
                    em.wait("vector", "SA", self.proj_all[s - 1])
                    em.wait("vector", "SS", self.xnt_spill)
                for d4 in range(4):
                    dchunk = grp4 * 4 + d4
                    em.do("vector", "tensor_copy",
                          out=self.xnt[0][:, dchunk * SUP + tt * 128:
                                          dchunk * SUP + (tt + 1) * 128],
                          in_=self.PB[:, 1024:1280].bitcast(BF16)[:, d4 * 128:
                                                                  (d4 + 1) * 128],
                          sem=("SV" if d4 == 3 else None))
                self.pb_ln_free = em.cnt["SV"]
            self.trp_last[gi] = self.trp_done[(gi, 3)]
            ld_tile(gi + 2)
        self.xnt_done[s] = em.cnt["SV"]

        for sl in range(NSL):
            for woff, kindb in ((0, "b"), (16, "a")):
                em.wait("tensor", "SV", self.xnt_done[s])
                em.wait("tensor", "SW", self.WBA_SW)
                em.wait("tensor", "SA", self.slot_free[0])
                for dchunk in range(16):
                    em.do("tensor", "matmul", out=self.PA[0:16, 0:512],
                          lhsT=self.wbas[:, dchunk * 32 + woff:
                                         dchunk * 32 + woff + 16],
                          rhs=self.xnt[par][:, dchunk * SUP + sl * TSL:
                                            dchunk * SUP + (sl + 1) * TSL],
                          start=(dchunk == 0), stop=(dchunk == 15),
                          sem=("SP" if dchunk == 15 else None))
                pv = em.cnt["SP"]
                tok0 = s * SUP + sl * TSL
                em.wait("scalar", "SP", pv)
                if kindb == "b":
                    self.slot_free[0] = em.do(
                        "scalar", "activation",
                        out=self.bgrows[32:48, tok0:tok0 + TSL],
                        in_=self.PA[0:16, 0:512], func=AF.Sigmoid, sem="SA")
                else:
                    self.slot_free[0] = em.do(
                        "scalar", "copy", out=self.bgrows[0:16, tok0:tok0 + TSL],
                        in_=self.PA[0:16, 0:512], sem="SA")

        self.a1_units(s, par)

        if s == 5:
            em.wait("vector", "SV", self.xnt_done[5])
            for d in range(16):
                em.do("vector", "tensor_copy",
                      out=self.xtail[:, d * 3:(d + 1) * 3],
                      in_=self.xnt[0][:, d * SUP + SUP - 3:d * SUP + SUP],
                      sem=("SV" if d == 15 else None))
            em.vsync()
            self.xtail_done = em.cnt["SV"]

        if s >= 6:
            em.wait("sync", "SV", self.xnt_done[s])
            em.do("sync", "dma_start",
                  out=self.scr_xnT[:, :, (s - 6) * SUP:(s - 5) * SUP].rearrange(
                      "d p t -> p d t"),
                  in_=self.xnt[par][:].rearrange("p (d t) -> p d t", d=16),
                  sem="SS", k=16)
            self.xnt_spill = em.cnt["SS"]

    def prime_q_tails(self, g, gidx):
        em = self.em
        buf = self.wgrp[gidx % 2]
        em.wait("tensor", "SW", self.wl_loaded_g[gidx])
        for i4 in range(4):
            cc = 32 + g * 4 + i4
            em.wait("tensor", "SV", self.TAILS_SV)
            em.wait("tensor", "SV", self.xtail_done)
            em.wait("tensor", "SA", self.pb_l2_sa)
            for dchunk in range(16):
                em.do("tensor", "matmul", out=self.PB[:, 1536:1539],
                      lhsT=buf[:, dchunk * 512 + i4 * 128:
                               dchunk * 512 + (i4 + 1) * 128],
                      rhs=self.xtail[:, dchunk * 3:(dchunk + 1) * 3],
                      start=(dchunk == 0), stop=(dchunk == 15),
                      sem=("SP" if dchunk == 15 else None))
            pv = em.cnt["SP"]
            em.wait("scalar", "SP", pv)
            em.do("scalar", "copy", out=self.tails[:, cc * 3:cc * 3 + 3],
                  in_=self.PB[:, 1536:1539])
            self.pb_l2_sa = em.do(
                "scalar", "copy", out=self.lncol[:, 40:41],
                in_=self.tails[:, cc * 3 + 2:cc * 3 + 3], sem="SA")
        self.TAILS_SA = self.pb_l2_sa

    def build_diag(self, kind, g, gidx):
        em = self.em
        dpar = gidx % 2
        cc0 = {"kv": g * 4, "q": 32 + g * 4}[kind]
        em.wait("vector", "SP", self.diag_used[dpar])
        em.wait("vector", "SX", self.CWC_SX)
        for i4 in range(4):
            for j in range(KCV):
                cc = cc0 + i4
                em.do("vector", "tensor_scalar",
                      out=self.cwd[dpar][:, (i4 * 4 + j) * 128:
                                         (i4 * 4 + j + 1) * 128],
                      in0=self.identt[:],
                      scalar1=self.cwc[:, cc * 4 + j:cc * 4 + j + 1],
                      scalar2=None, op0=ALU.mult,
                      sem=("SV" if (i4, j) == (3, 3) else None))
        self.diag_built[dpar] = em.cnt["SV"]

    def a1_units(self, s, par):
        em = self.em
        wl = self.wloads_for(s)
        gbase = sum(len(self.wloads_for(s2)) for s2 in range(s))
        if s == 0:
            self.load_w(0)
            self.load_w(1)
        q = []

        def tick(u):
            q.append(u)
            n = len(q)
            if u is not None:
                self.u_proj(u)
            if n >= 4 and q[n - 4] is not None:
                self.u_store(q[n - 4])
            if n >= 2 and q[n - 2] is not None:
                self.u_conv(q[n - 2])
            if n >= 3 and q[n - 3] is not None:
                v = q[n - 3]
                self.u_norm(v)
                if v["glast"] is not None:
                    self.wl_free_g[v["glast"]] = em.cnt["SA"]
                    self.load_w(v["glast"] + 2)
            if u is not None:
                self.u_evac(u)

        def drain():
            while any(x is not None for x in q[-3:]) and q:
                tick(None)

        for wi, (kind, g) in enumerate(wl):
            gidx = gbase + wi
            if kind != "z":
                self.build_diag(kind, g, gidx)
            if kind == "q" and s == 6:
                drain()
                self.prime_q_tails(g, gidx)
            pos = [(sl, i4) for sl in range(NSL) for i4 in range(4)]
            for j, (sl, i4) in enumerate(pos):
                if kind == "z":
                    cls = "z"
                elif kind == "q" or g < 4:
                    cls = "k"
                else:
                    cls = "v"
                u = dict(kind=kind, g=g, gidx=gidx, sl=sl, i4=i4, s=s, par=par,
                         cls=cls, slot=self.ucnt % 4,
                         glast=(gidx if j == len(pos) - 1 else None))
                self.ucnt += 1
                if cls != "z":
                    u["cpar"] = self.ccnt % 2
                    self.ccnt += 1
                if cls == "k":
                    u["bp"] = self.kqcnt % 2
                    u["kp"] = self.kqcnt % 3
                    self.kqcnt += 1
                else:
                    u["vp"] = self.vzcnt % 2
                    u["vs"] = self.vzcnt % 3
                    self.vzcnt += 1
                tick(u)
        drain()
        self.proj_all[s] = em.cnt["SA"]

    def u_proj(self, u):
        em = self.em
        slot = u["slot"]
        pslc = self.PA[:, slot * 512:(slot + 1) * 512]
        u["pslc"] = pslc
        buf = self.wgrp[u["gidx"] % 2]
        tok0 = u["sl"] * TSL
        i4 = u["i4"]
        em.wait("tensor", "SW", self.wl_loaded_g[u["gidx"]])
        em.wait("tensor", "SV", self.xnt_done[u["s"]])
        em.wait("tensor", "SA", self.slot_free[slot])
        em.wait("tensor", "SV", self.slot_evsv[slot])
        for dchunk in range(16):
            em.do("tensor", "matmul", out=pslc,
                  lhsT=buf[:, dchunk * 512 + i4 * 128:
                           dchunk * 512 + (i4 + 1) * 128],
                  rhs=self.xnt[u["par"]][:, dchunk * SUP + tok0:
                                         dchunk * SUP + tok0 + TSL],
                  start=(dchunk == 0), stop=(dchunk == 15),
                  sem=("SP" if dchunk == 15 else None))
        u["projv"] = em.cnt["SP"]

    def u_evac(self, u):
        if u["cls"] == "z":
            return
        em = self.em
        slot = u["slot"]
        em.wait("vector", "SP", u["projv"])
        em.do("vector", "tensor_copy",
              out=self.mxb[:, slot * TSL:(slot + 1) * TSL], in_=u["pslc"],
              sem="SV")
        u["mev"] = em.cnt["SV"]
        self.slot_evsv[slot] = u["mev"]

    def u_conv(self, u):
        em = self.em
        if u["cls"] == "z":
            vp = u["vp"]
            em.wait("scalar", "SP", u["projv"])
            em.wait("scalar", "SP", self.vsil_free[vp])
            self.slot_free[u["slot"]] = em.do(
                "scalar", "activation", out=self.vsil[vp][:], in_=u["pslc"],
                func=AF.Silu, sem="SA")
            u["siluv"] = self.slot_free[u["slot"]]
            return
        slot, i4, g, kind = u["slot"], u["i4"], u["g"], u["kind"]
        cpar = u["cpar"]
        cslc = self.PB[:, cpar * 512:(cpar + 1) * 512]
        u["cslc"] = cslc
        dpar = u["gidx"] % 2
        cc = {"kv": g * 4 + i4 if g < 4 else 16 + (g - 4) * 4 + i4,
              "q": 32 + g * 4 + i4}[kind]
        u["cc"] = cc
        em.wait("tensor", "SV", u["mev"])
        em.wait("tensor", "SV", self.diag_built[dpar])
        em.wait("tensor", "SV", self.TAILS_SV)
        em.wait("tensor", "SA", self.TAILS_SA)
        em.wait("tensor", "SA", self.conv_free[cpar])
        mms = []
        for j in range(KCV):
            mms.append((cslc[:, 3 - j:512],
                        self.cwd[dpar][:, (i4 * 4 + j) * 128:
                                       (i4 * 4 + j + 1) * 128],
                        self.mxb[:, slot * TSL:slot * TSL + 509 + j]))
            if j < 3:
                mms.append((cslc[:, 0:3 - j],
                            self.cwd[dpar][:, (i4 * 4 + j) * 128:
                                           (i4 * 4 + j + 1) * 128],
                            self.tails[:, cc * 3 + j:cc * 3 + 3]))
        for mi, (o_, l_, r_) in enumerate(mms):
            em.do("tensor", "matmul", out=o_, lhsT=l_, rhs=r_,
                  start=(mi == 0), stop=(mi == len(mms) - 1),
                  sem=("SP" if mi == len(mms) - 1 else None))
        u["convv"] = em.cnt["SP"]
        self.diag_used[dpar] = u["convv"]
        em.wait("vector", "SP", u["convv"])
        em.do("vector", "tensor_copy", out=self.tails[:, cc * 3:cc * 3 + 3],
              in_=self.mxb[:, slot * TSL + 509:slot * TSL + 512], sem="SV")
        self.TAILS_SV = em.cnt["SV"]
        if u["cls"] == "k":
            bp = u["bp"]
            kp = u["kp"]
            em.wait("scalar", "SP", u["convv"])
            em.wait("scalar", "SV", self.ksil_free[kp])
            self.conv_free[cpar] = em.do(
                "scalar", "activation", out=self.ksil[kp][:], in_=cslc,
                func=AF.Silu, sem="SA")
            em.wait("scalar", "SP", self.sqb_free[bp])
            em.do("scalar", "activation", out=self.sqb[bp][:],
                  in_=self.ksil[kp][:], func=AF.Square, sem="SA")
            u["sqv"] = em.cnt["SA"]

    def u_norm(self, u):
        em = self.em
        if u["cls"] == "z":
            return
        if u["cls"] == "v":
            vp = u["vp"]
            em.wait("scalar", "SP", u["convv"])
            em.wait("scalar", "SP", self.vsil_free[vp])
            self.conv_free[u["cpar"]] = em.do(
                "scalar", "activation", out=self.vsil[vp][:], in_=u["cslc"],
                func=AF.Silu, sem="SA")
            u["siluv"] = self.conv_free[u["cpar"]]
            return
        bp = u["bp"]
        em.wait("tensor", "SA", u["sqv"])
        em.wait("tensor", "SA", self.pb_l2_sa)
        em.do("tensor", "matmul", out=self.PB[:, 1536:2048], lhsT=self.onesb[:],
              rhs=self.sqb[bp][:], start=True, stop=True, sem="SP")
        onev = em.cnt["SP"]
        self.sqb_free[bp] = onev
        em.wait("scalar", "SP", onev)
        em.wait("scalar", "SV", self.l2s_free[bp])
        scale = float(DK) if u["kind"] == "q" else 1.0
        epsc = self.cst[:, 1:2] if u["kind"] == "q" else self.cst[:, 0:1]
        em.do("scalar", "activation", out=self.l2s[bp][:],
              in_=self.PB[:, 1536:2048], func=AF.Sqrt, scale=scale, bias=epsc)
        self.pb_l2_sa = em.do(
            "scalar", "copy", out=self.lncol[:, 41:42],
            in_=self.l2s[bp][:, TSL - 1:TSL], sem="SA")
        u["sqrtv"] = self.pb_l2_sa

    def u_store(self, u):
        em = self.em
        c0 = u["s"] * NTIL + u["sl"] * 4
        if u["cls"] == "k":
            bp, kp = u["bp"], u["kp"]
            em.wait("vector", "SA", u["sqrtv"])
            em.do("vector", "reciprocal", out=self.l2s[bp][:],
                  in_=self.l2s[bp][:])
            em.vsync()
            em.wait("vector", "SS", self.kstg_free[kp])
            em.do("vector", "tensor_mul", out=self.kstg[kp][:],
                  in0=self.ksil[kp][:], in1=self.l2s[bp][:], sem="SV")
            mv = em.cnt["SV"]
            self.ksil_free[kp] = mv
            self.l2s_free[bp] = mv
            if u["kind"] == "q":
                target, cbase = self.scr_q, c0 - OWNCH0
            else:
                target, cbase = self.scr_k, c0
            h = (u["g"] % 4) * 4 + u["i4"]
            em.wait("sync", "SV", mv)
            self.kstg_free[kp] = em.do(
                "sync", "dma_start",
                out=target[cbase:cbase + 4, :, h * 128:(h + 1) * 128].rearrange(
                    "c p t -> p c t"),
                in_=self.kstg[kp][:].rearrange("p (c t) -> p c t", c=4),
                sem="SS", k=16)
            return
        vp, vs = u["vp"], u["vs"]
        if u["cls"] == "v":
            target, cbase = self.scr_v, c0
            h = (u["g"] - 4) * 4 + u["i4"]
        else:
            target, cbase = self.scr_z, c0 - OWNCH0
            h = u["g"] * 4 + u["i4"]
        em.wait("tensor", "SA", u["siluv"])
        em.wait("tensor", "SV", self.pb_tr_free)
        for t4 in range(4):
            em.do("tensor", "transpose",
                  out=self.PB[:, 1280:1536].bitcast(BF16)[:, t4 * 128:
                                                          (t4 + 1) * 128],
                  in_=self.vsil[vp][:, t4 * 128:(t4 + 1) * 128],
                  identity=self.identt[:], sem=("SP" if t4 == 3 else None))
        tv = em.cnt["SP"]
        self.vsil_free[vp] = tv
        em.wait("vector", "SP", tv)
        em.wait("vector", "SS", self.vstg_free[vs])
        em.do("vector", "tensor_copy", out=self.vstg[vs][:],
              in_=self.PB[:, 1280:1536].bitcast(BF16)[:, 0:512], sem="SV")
        cv = em.cnt["SV"]
        self.pb_tr_free = cv
        em.wait("sync", "SV", cv)
        self.vstg_free[vs] = em.do(
            "sync", "dma_start",
            out=target[cbase:cbase + 4, :, h * 128:(h + 1) * 128].rearrange(
                "c p d -> p c d"),
            in_=self.vstg[vs][:].rearrange("p (c d) -> p c d", c=4),
            sem="SS", k=16)

    def a2_rows(self):
        em = self.em
        bg = self.bgrows
        rA, rB = self.rowsA, self.rowsB
        em.wait("vector", "SA")
        em.wait("scalar", "SA")
        em.wait("scalar", "SV")
        em.do("vector", "tensor_scalar", out=bg[64:80, :], in0=bg[0:16, :],
              scalar1=self.adts[:, 1:2], scalar2=None, op0=ALU.add,
              sem="SV")
        em.wait("scalar", "SV")
        em.do("scalar", "activation", out=rA[64:80, :], in_=bg[64:80, :],
              func=AF.Abs)
        em.do("scalar", "activation", out=rA[64:80, :], in_=rA[64:80, :],
              func=AF.Exp, scale=-1.0)
        em.do("scalar", "activation", out=rA[64:80, :], in_=rA[64:80, :],
              func=AF.Ln, bias=self.cst[0:16, 2:3], sem="SA")
        em.wait("vector", "SA")
        em.do("vector", "tensor_scalar", out=bg[64:80, :], in0=bg[64:80, :],
              scalar1=0.0, scalar2=None, op0=ALU.max)
        em.vsync()
        em.do("vector", "tensor_add", out=bg[64:80, :], in0=bg[64:80, :],
              in1=rA[64:80, :])
        em.vsync()
        em.do("vector", "tensor_scalar", out=bg[0:16, :], in0=bg[64:80, :],
              scalar1=self.negea[64:80, :], scalar2=None, op0=ALU.mult,
              sem="SV")
        gv = em.cnt["SV"]
        em.wait("vector", "SV", gv)
        em.vsync()
        for c in range(NCH):
            em.do("vector", "tensor_tensor_scan",
                  out=rA[0:16, c * C:(c + 1) * C], data0=self.onesr16[:],
                  data1=bg[0:16, c * C:(c + 1) * C], initial=0.0,
                  op0=ALU.mult, op1=ALU.add,
                  sem=("SV" if c == NCH - 1 else None))
        Gv = em.cnt["SV"]
        em.wait("scalar", "SV", Gv)
        em.do("scalar", "activation", out=bg[0:16, :], in_=bg[32:48, :],
              func=AF.Ln, sem="SA")
        em.wait("vector", "SA")
        em.vsync()
        em.do("vector", "tensor_add", out=rA[32:48, :], in0=rA[0:16, :],
              in1=bg[0:16, :])
        em.do("vector", "tensor_copy", out=rB[0:16, :], in_=bg[32:48, :], sem="SV")
        for c in range(NCH):
            em.do("vector", "tensor_scalar", out=bg[64:80, c * C:(c + 1) * C],
                  in0=rA[0:16, c * C:(c + 1) * C],
                  scalar1=rA[0:16, c * C + C - 1:c * C + C],
                  scalar2=None, op0=ALU.subtract,
                  sem=("SV" if c == NCH - 1 else None))
        em.wait("scalar", "SV")
        em.do("scalar", "activation", out=rB[32:48, :], in_=bg[64:80, :],
              func=AF.Exp, scale=-1.0, sem="SA")
        if "rows" in self.taps:
            em.wait("sync", "SA")
            em.wait("sync", "SV")
            em.do("sync", "dma_start", out=self.dbg_rows[:, 0:NT], in_=rA[:],
                  sem="SS", k=16)
            em.wait("gpsimd", "SA")
            em.wait("gpsimd", "SV")
            em.do("gpsimd", "dma_start",
                  out=self.dbg_rows[0:64, NT:NT + NT], in_=rB[0:64, :],
                  sem="SS", k=16)

    def a3_alloc(self):
        sb2 = lambda n, s_, d: self.es2.enter_context(self.nc.sbuf_tensor(n, s_, d))
        self.Sst = sb2("Sst", [128, H * 128], F32)
        self.SbfT = sb2("SbfT", [128, H * 128], BF16)
        self.kT = [sb2(f"kT{i}", [128, 2048], BF16) for i in range(2)]
        self.vT = [sb2(f"vT{i}", [128, 2048], BF16) for i in range(2)]
        self.qT = [sb2(f"qT{i}", [128, 2048], BF16) for i in range(2)]
        self.zT = [sb2(f"zT{i}", [128, 2048], BF16) for i in range(2)]
        self.w0 = sb2("w0", [128, 2048], F32)
        self.w1 = sb2("w1", [128, 2048], F32)
        self.w2 = sb2("w2", [128, 2048], F32)
        self.uf = sb2("uf", [128, 2048], F32)
        self.bb = [sb2(f"bb{i}", [128, 2048], BF16) for i in range(9)]
        self.colsA = sb2("colsA", [128, 128], F32)
        self.prow = [sb2(f"prow{i}", [1, 4096], F32) for i in range(2)]
        self.gatedb = sb2("gatedb", [128, 2048], BF16)

    def a3_scan(self):
        em = self.em
        PA, PB = self.PA, self.PB
        em.join()
        em.do("vector", "memset", self.Sst[:], 0.0)
        em.do("vector", "memset", self.SbfT[:], 0.0, sem="SV")
        self.ld_ck = {}
        self.ck_done = {}
        self.ck_done_sv = {}
        self.pbq_free = 0
        self.d1_free = 0
        self.pbA_free = 0
        self.pa_last = 0
        self.pb_last = 0
        self.a3_load(0)
        for c in range(NCH):
            self.a3_chunk(c)

    def a3_load(self, c):
        if c >= NCH:
            return
        em = self.em
        par = c % 2
        key = c - 2
        if key >= 0:
            em.wait("scalar", "SP", self.ck_done.get(key, 0))
            em.wait("scalar", "SV", self.ck_done_sv.get(key, 0))
        n = em.do("scalar", "dma_start", out=self.kT[par][:], in_=self.scr_k[c],
                  sem="SL", k=16)
        n = em.do("scalar", "dma_start", out=self.vT[par][:], in_=self.scr_v[c],
                  sem="SL", k=16)
        if c >= OWNCH0:
            n = em.do("scalar", "dma_start", out=self.qT[par][:],
                      in_=self.scr_q[c - OWNCH0], sem="SL", k=16)
            n = em.do("scalar", "dma_start", out=self.zT[par][:],
                      in_=self.scr_z[c - OWNCH0], sem="SL", k=16)
        n = em.do("scalar", "dma_start",
                  out=self.prow[par][0:1, 0:2048],
                  in_=self.rowsA[32:48, c * C:(c + 1) * C], sem="SL", k=16)
        if c >= OWNCH0:
            n = em.do("scalar", "dma_start",
                      out=self.prow[par][0:1, 2048:4096],
                      in_=self.rowsA[0:16, c * C:(c + 1) * C], sem="SL", k=16)
        self.ld_ck[c] = em.cnt["SL"]

    def a3_chunk(self, c):
        em = self.em
        PA, PB = self.PA, self.PB
        own = c >= OWNCH0
        par = c % 2
        kT, vT, qT, zT = (self.kT[par], self.vT[par], self.qT[par], self.zT[par])
        cA = self.colsA
        self.a3_load(c + 1)

        def heads(ap, h):
            return ap[:, h * 128:(h + 1) * 128]

        def stretch(sl, nh=16, off=0):
            return cA[:, sl * 16 + off:sl * 16 + off + nh].unsqueeze(2)                 .broadcast_to([128, nh, 128])

        def wide(ap, nh=16, off=0):
            return ap[:, off * 128:(off + nh) * 128].rearrange(
                "p (h d) -> p h d", h=nh)

        if c == 0:
            em.wait("vector", "SL", self.ld_ck[0])
            em.do("vector", "tensor_copy", out=self.bb[7][:], in_=kT[:], sem="SV")
            em.do("vector", "tensor_copy", out=self.bb[7][:], in_=vT[:], sem="SV")
        em.wait("tensor", "SL", self.ld_ck[c])
        em.wait("tensor", "SV", self.pb_last)
        em.do("tensor", "transpose", out=PB[:, 1792:1808],
              in_=self.rowsA[0:16, c * C:(c + 1) * C],
              identity=self.identf[0:16, 0:16])
        em.do("tensor", "transpose",
              out=PB[:, 1808:1816].bitcast(BF16)[:, 0:16],
              in_=self.rowsB[0:16, c * C:(c + 1) * C],
              identity=self.identt[0:16, 0:16])
        em.do("tensor", "transpose",
              out=PB[:, 1816:1824].bitcast(BF16)[:, 0:16],
              in_=self.rowsB[32:48, c * C:(c + 1) * C],
              identity=self.identt[32:48, 32:48], sem="SP")
        trv = em.cnt["SP"]
        em.wait("vector", "SP", trv)
        em.do("vector", "tensor_copy", out=cA[:, 0:16], in_=PB[:, 1792:1808])
        em.do("vector", "tensor_copy", out=cA[:, 16:32],
              in_=PB[:, 1808:1816].bitcast(BF16)[:, 0:16])
        em.do("vector", "tensor_copy", out=cA[:, 32:48],
              in_=PB[:, 1816:1824].bitcast(BF16)[:, 0:16], sem="SV")
        self.pbq_free = em.cnt["SV"]
        em.vsync()
        em.wait("scalar", "SV")
        em.do("scalar", "activation", out=cA[:, 48:64], in_=cA[:, 0:16],
              func=AF.Exp)
        em.do("scalar", "copy", out=cA[:, 112:113], in_=cA[:, 63:64], sem="SA")
        em.wait("vector", "SA")
        em.do("vector", "tensor_mul", out=cA[:, 64:80], in0=cA[:, 16:32],
              in1=cA[:, 48:64])
        em.do("vector", "tensor_mul", out=cA[:, 80:96], in0=cA[:, 32:48],
              in1=cA[:, 48:64], sem="SV")
        em.vsync()
        colv = em.cnt["SV"]

        em.wait("tensor", "SL", self.ld_ck[c])
        em.wait("tensor", "SV", self.pa_last)
        for i in range(4):
            em.do("tensor", "matmul", out=PA[:, i * 512:(i + 1) * 512],
                  lhsT=self.onesrowf[:],
                  rhs=self.prow[par][0:1, i * 512:(i + 1) * 512],
                  start=True, stop=True, sem=("SP" if i == 3 else None))
        bv = em.cnt["SP"]
        em.wait("vector", "SP", bv)
        em.wait("vector", "SV", colv)
        em.do("vector", "scalar_tensor_tensor",
              out=self.w0[:].rearrange("p (h d) -> p h d", h=16),
              in0=stretch(0), scalar=-1.0,
              in1=PA[:].rearrange("p (h d) -> p h d", h=16),
              op0=ALU.mult, op1=ALU.add, sem="SV")
        d1bv = em.cnt["SV"]
        if own:
            em.wait("tensor", "SV", d1bv)
            for i in range(4):
                em.do("tensor", "matmul", out=PA[:, i * 512:(i + 1) * 512],
                      lhsT=self.onesrowf[:],
                      rhs=self.prow[par][0:1, 2048 + i * 512:2048 + (i + 1) * 512],
                      start=True, stop=True, sem=("SP" if i == 3 else None))
            bv2 = em.cnt["SP"]
            em.wait("vector", "SP", bv2)
            em.do("vector", "scalar_tensor_tensor",
                  out=self.w1[:].rearrange("p (h d) -> p h d", h=16),
                  in0=stretch(0), scalar=-1.0,
                  in1=PA[:].rearrange("p (h d) -> p h d", h=16),
                  op0=ALU.mult, op1=ALU.add, sem="SV")
        self.d1_free = em.cnt["SV"]

        em.wait("tensor", "SV", self.pbq_free)
        for h in range(H):
            em.do("tensor", "matmul", out=PB[:, h * 128:(h + 1) * 128],
                  lhsT=heads(kT, h), rhs=heads(kT, h), start=True, stop=True,
                  sem=("SP" if h == H - 1 else None))
        Av = em.cnt["SP"]
        em.do("vector", "tensor_add", out=self.w2[:].rearrange(
            "p (h d) -> p h d", h=16), in0=self.w0[:].rearrange(
            "p (h d) -> p h d", h=16),
            in1=self.maskSt[:].unsqueeze(1).broadcast_to([128, 16, 128]), sem="SV")
        em.wait("scalar", "SV")
        em.do("scalar", "activation", out=self.bb[0][:], in_=self.w2[:],
              func=AF.Exp, sem="SA")
        em.wait("vector", "SA")
        em.wait("vector", "SP", Av)
        em.do("vector", "scalar_tensor_tensor", out=self.bb[1][:],
              in0=self.bb[0][:], scalar=-1.0, in1=PB[:],
              op0=ALU.mult, op1=ALU.mult, sem="SV")
        ntv = em.cnt["SV"]
        if own:
            em.wait("tensor", "SV", ntv)
            for h in range(H):
                em.do("tensor", "matmul", out=PB[:, h * 128:(h + 1) * 128],
                      lhsT=heads(kT, h), rhs=heads(qT, h), start=True, stop=True,
                      sem=("SP" if h == H - 1 else None))
            KQv = em.cnt["SP"]
            em.do("vector", "tensor_add", out=self.w2[:].rearrange(
                "p (h d) -> p h d", h=16), in0=self.w1[:].rearrange(
                "p (h d) -> p h d", h=16),
                in1=self.maskIt[:].unsqueeze(1).broadcast_to([128, 16, 128]), sem="SV")
            em.wait("scalar", "SV")
            em.do("scalar", "activation", out=self.bb[0][:], in_=self.w2[:],
                  func=AF.Exp, sem="SA")
            em.wait("vector", "SA")
            em.wait("vector", "SP", KQv)
            em.do("vector", "tensor_mul", out=self.bb[2][:], in0=self.bb[0][:],
                  in1=PB[:], sem="SV")
        pbAv = em.cnt["SV"]
        self.pbA_free = pbAv

        em.wait("tensor", "SV", pbAv)
        for h in range(H):
            em.do("tensor", "transpose",
                  out=PA[:, 0:1024].bitcast(BF16)[:, h * 128:(h + 1) * 128],
                  in_=heads(self.bb[1], h), identity=self.identt[:],
                  sem=("SP" if h == H - 1 else None))
        Ntr = em.cnt["SP"]
        em.wait("vector", "SP", Ntr)
        em.do("vector", "tensor_copy", out=self.bb[3][:],
              in_=PA[:, 0:1024].bitcast(BF16)[:, 0:2048], sem="SV")
        Nv = em.cnt["SV"]

        em.wait("tensor", "SV", Nv)
        for h in range(H):
            em.do("tensor", "matmul", out=PB[:, h * 128:(h + 1) * 128],
                  lhsT=heads(self.bb[1], h), rhs=heads(self.bb[3], h),
                  start=True, stop=True, sem=("SP" if h == H - 1 else None))
        P2v = em.cnt["SP"]
        em.wait("scalar", "SP", P2v)
        em.do("scalar", "copy", out=self.bb[4][:], in_=PB[:], sem="SA")
        em.wait("tensor", "SA")
        for h in range(H):
            em.do("tensor", "matmul", out=PA[:, h * 128:(h + 1) * 128],
                  lhsT=heads(self.bb[3], h), rhs=heads(self.bb[1], h),
                  start=True, stop=True, sem=("SP" if h == H - 1 else None))
        P2Tv = em.cnt["SP"]
        em.wait("scalar", "SP", P2Tv)
        em.do("scalar", "copy", out=self.bb[5][:], in_=PA[:], sem="SA")
        em.do("vector", "tensor_add",
              out=self.bb[6][:].rearrange("p (h d) -> p h d", h=16),
              in0=self.bb[1][:].rearrange("p (h d) -> p h d", h=16),
              in1=self.identt[:].unsqueeze(1).broadcast_to([128, 16, 128]), sem="SV")
        em.vsync()
        MTv = em.cnt["SV"]
        em.wait("tensor", "SA")
        em.wait("tensor", "SV", MTv)
        for h in range(H):
            em.do("tensor", "matmul", out=PB[:, h * 128:(h + 1) * 128],
                  lhsT=heads(self.bb[4], h), rhs=heads(self.bb[6], h),
                  start=True, stop=True, sem=("SP" if h == H - 1 else None))
        u1v = em.cnt["SP"]
        em.wait("vector", "SP", u1v)
        em.do("vector", "tensor_add", out=self.bb[6][:], in0=self.bb[6][:],
              in1=PB[:], sem="SV")
        em.vsync()
        em.wait("tensor", "SA")
        for h in range(H):
            em.do("tensor", "matmul", out=PA[:, h * 128:(h + 1) * 128],
                  lhsT=heads(self.bb[5], h), rhs=heads(self.bb[4], h),
                  start=True, stop=True, sem=("SP" if h == H - 1 else None))
        P4v = em.cnt["SP"]
        em.wait("scalar", "SP", P4v)
        em.do("scalar", "copy", out=self.bb[5][:], in_=PA[:], sem="SA")
        em.wait("tensor", "SA")
        em.wait("tensor", "SV")
        for h in range(H):
            em.do("tensor", "matmul", out=PB[:, h * 128:(h + 1) * 128],
                  lhsT=heads(self.bb[5], h), rhs=heads(self.bb[6], h),
                  start=True, stop=True, sem=("SP" if h == H - 1 else None))
        u2v = em.cnt["SP"]
        em.wait("vector", "SP", u2v)
        em.do("vector", "tensor_add", out=self.bb[6][:], in0=self.bb[6][:],
              in1=PB[:], sem="SV")
        em.vsync()
        MTfin = em.cnt["SV"]

        em.wait("tensor", "SV", MTfin)
        for h in range(H):
            em.do("tensor", "matmul", out=PA[:, h * 128:(h + 1) * 128],
                  lhsT=heads(kT, h), rhs=heads(self.SbfT, h), start=True,
                  stop=True, sem=("SP" if h == H - 1 else None))
        KSv = em.cnt["SP"]
        em.wait("vector", "SP", KSv)
        em.do("vector", "tensor_tensor",
              out=self.uf[:].rearrange("p (h d) -> p h d", h=16),
              in0=PA[:].rearrange("p (h d) -> p h d", h=16),
              in1=stretch(3), op=ALU.mult, sem="SV")
        em.vsync()
        em.do("vector", "tensor_sub", out=self.uf[:], in0=vT[:], in1=self.uf[:],
              sem="SV")
        em.vsync()
        em.do("vector", "tensor_tensor",
              out=self.bb[8][:].rearrange("p (h d) -> p h d", h=16),
              in0=self.uf[:].rearrange("p (h d) -> p h d", h=16),
              in1=stretch(1), op=ALU.mult, sem="SV")
        ubv = em.cnt["SV"]

        em.wait("tensor", "SV", ubv)
        for h in range(H):
            em.do("tensor", "matmul", out=PB[:, h * 128:(h + 1) * 128],
                  lhsT=heads(self.bb[6], h), rhs=heads(self.bb[8], h),
                  start=True, stop=True, sem=("SP" if h == H - 1 else None))
        dv = em.cnt["SP"]
        em.wait("scalar", "SP", dv)
        em.do("scalar", "copy", out=self.bb[0][:], in_=PB[:], sem="SA")
        dbv = em.cnt["SA"]

        if own:
            em.wait("tensor", "SV", ubv)
            for h in range(H):
                em.do("tensor", "matmul", out=PA[:, h * 128:(h + 1) * 128],
                      lhsT=heads(qT, h), rhs=heads(self.SbfT, h), start=True,
                      stop=True, sem=("SP" if h == H - 1 else None))
            QSv = em.cnt["SP"]
            em.wait("vector", "SP", QSv)
            em.do("vector", "tensor_tensor",
                  out=self.w0[:].rearrange("p (h d) -> p h d", h=16),
                  in0=PA[:].rearrange("p (h d) -> p h d", h=16),
                  in1=stretch(3), op=ALU.mult, sem="SV")
            Ofv = em.cnt["SV"]
            em.wait("tensor", "SA", dbv)
            em.wait("tensor", "SV", Ofv)
            for h in range(H):
                em.do("tensor", "matmul", out=PB[:, h * 128:(h + 1) * 128],
                      lhsT=heads(self.bb[2], h), rhs=heads(self.bb[0], h),
                      start=True, stop=True, sem=("SP" if h == H - 1 else None))
            Adv = em.cnt["SP"]
            em.wait("vector", "SP", Adv)
            em.vsync()
            em.do("vector", "tensor_add", out=self.w0[:], in0=self.w0[:],
                  in1=PB[:], sem="SV")
            em.vsync()
            Ov = em.cnt["SV"]

        em.wait("tensor", "SA", dbv)
        em.wait("tensor", "SV")
        for h in range(H):
            em.do("tensor", "transpose",
                  out=PA[:, 0:1024].bitcast(BF16)[:, h * 128:(h + 1) * 128],
                  in_=heads(kT, h), identity=self.identt[:],
                  sem=("SP" if h == H - 1 else None))
        ktr = em.cnt["SP"]
        em.wait("vector", "SP", ktr)
        em.do("vector", "tensor_tensor",
              out=self.bb[1][:].rearrange("p (h d) -> p h d", h=16),
              in0=PA[:, 0:1024].bitcast(BF16)[:, 0:2048].rearrange(
                  "p (h d) -> p h d", h=16),
              in1=stretch(2), op=ALU.mult, sem="SV")
        Kpv = em.cnt["SV"]

        em.wait("tensor", "SV", Kpv)
        for h in range(H):
            em.do("tensor", "matmul", out=PB[:, h * 128:(h + 1) * 128],
                  lhsT=heads(self.bb[1], h), rhs=heads(self.bb[0], h),
                  start=True, stop=True, sem=("SP" if h == H - 1 else None))
        KpDv = em.cnt["SP"]
        em.do("vector", "tensor_tensor",
              out=self.Sst[:].rearrange("p (h d) -> p h d", h=16),
              in0=self.Sst[:].rearrange("p (h d) -> p h d", h=16),
              in1=stretch(5), op=ALU.mult, sem="SV")
        em.vsync()
        em.wait("vector", "SP", KpDv)
        em.do("vector", "tensor_add", out=self.Sst[:], in0=self.Sst[:],
              in1=PB[:], sem="SV")
        em.vsync()
        em.do("vector", "tensor_copy", out=self.SbfT[:], in_=self.Sst[:],
              sem="SV")
        em.vsync()

        if own:
            em.wait("scalar", "SV", Ov)
            em.do("scalar", "activation", out=self.w1[:], in_=self.w0[:],
                  func=AF.Square, sem="SA")
            em.wait("vector", "SA")
            em.do("vector", "tensor_reduce",
                  out=cA[:, 96:112].unsqueeze(2),
                  in_=self.w1[:].rearrange("p (h d) -> p h d", h=16),
                  axis=AX.X, op=ALU.add, sem="SV")
            em.vsync()
            em.wait("scalar", "SV")
            em.do("scalar", "activation", out=cA[:, 96:112], in_=cA[:, 96:112],
                  func=AF.Sqrt, scale=1.0 / 128.0, bias=self.cst[:, 0:1])
            em.do("scalar", "copy", out=cA[:, 112:113], in_=cA[:, 111:112],
                  sem="SA")
            em.wait("vector", "SA")
            em.do("vector", "reciprocal", out=cA[:, 96:112], in_=cA[:, 96:112])
            em.vsync()
            em.do("vector", "tensor_tensor",
                  out=self.w1[:].rearrange("p (h d) -> p h d", h=16),
                  in0=self.w0[:].rearrange("p (h d) -> p h d", h=16),
                  in1=stretch(6), op=ALU.mult, sem="SV")
            em.vsync()
            em.do("vector", "tensor_tensor",
                  out=self.w1[:].rearrange("p (h d) -> p h d", h=16),
                  in0=self.w1[:].rearrange("p (h d) -> p h d", h=16),
                  in1=self.nwt[:].unsqueeze(1).broadcast_to([128, 16, 128]),
                  op=ALU.mult)
            em.vsync()
            em.wait("vector", "SS", getattr(self, "gspill", 0))
            em.do("vector", "tensor_mul", out=self.gatedb[:], in0=self.w1[:],
                  in1=zT[:], sem="SV")
            gv_ = em.cnt["SV"]
            em.wait("sync", "SV", gv_)
            self.gspill = em.do("sync", "dma_start", out=self.scr_g[c - OWNCH0],
                                in_=self.gatedb[:], sem="SS", k=16)
        if "ck" in self.taps and c == 48:
            em.wait("vector", "SA")
            em.wait("vector", "SV")
            em.vsync()
            for i, t_ in enumerate((self.bb[6], self.bb[0], None, self.bb[2],
                                    self.bb[1], self.bb[8])):
                if t_ is None:
                    continue
                em.do("vector", "tensor_copy", out=self.w2[:], in_=t_[:],
                      sem="SV")
                em.vsync()
                em.wait("sync", "SV")
                em.do("sync", "dma_start", out=self.dbg_ck[i], in_=self.w2[:],
                      sem="SS", k=16)
                em.wait("vector", "SS")
            em.do("sync", "dma_start", out=self.dbg_ck[2], in_=self.w0[:],
                  sem="SS", k=16)
            em.wait("vector", "SS")
        self.ck_done[c] = em.cnt["SP"]
        self.ck_done_sv[c] = em.cnt["SV"]
        self.pa_last = em.cnt["SV"]
        self.pb_last = em.cnt["SV"]

    def b_wout(self):
        em = self.em
        PA, PB = self.PA, self.PB
        sb3 = lambda n, s_, d: self.es3.enter_context(self.nc.sbuf_tensor(n, s_, d))
        wosb = sb3("wosb", [128, 16 * 2048], BF16)
        gld = [sb3(f"gld{i}", [128, 2048], BF16) for i in range(2)]
        gld2 = sb3("gld2", [128, 2048], BF16)
        gtT = sb3("gtT", [128, 2048], BF16)
        ssb = [sb3(f"ssb{i}", [128, 2048], BF16) for i in range(2)]
        em.do("gpsimd", "dma_start",
              out=wosb[:].rearrange("p (r c) -> p r c", r=16),
              in_=self.wo.rearrange("r p c -> p r c"), sem="SW", k=16)
        em.wait("vector", "SW")
        for i in range(16):
            em.do("vector", "tensor_copy", out=gld2[:],
                  in_=wosb[:, i * 2048:(i + 1) * 2048],
                  sem=("SV" if i == 15 else None))
        wov = em.cnt["SV"]
        g_ld = {}
        s_done = {}
        pa_tr_free = 0
        gtT_free = 0
        gld2_free = 0
        for cp in range(NOC):
            par = cp % 2
            if cp == 0 or cp == 1:
                pass
            if cp == 0:
                g_ld[0] = em.do("scalar", "dma_start", out=gld[0][:],
                                in_=self.scr_g[0], sem="SL", k=16)
            if cp + 1 < NOC:
                if cp - 1 >= 0:
                    em.wait("scalar", "SP", s_done.get(cp - 1, 0))
                g_ld[cp + 1] = em.do("scalar", "dma_start", out=gld[(cp + 1) % 2][:],
                                     in_=self.scr_g[cp + 1], sem="SL", k=16)
            em.wait("vector", "SL", g_ld[cp])
            em.wait("vector", "SP", gld2_free)
            em.do("vector", "tensor_copy", out=gld2[:], in_=gld[par][:], sem="SV")
            g2v = em.cnt["SV"]
            em.wait("tensor", "SV", g2v)
            em.wait("tensor", "SV", pa_tr_free)
            em.wait("tensor", "SP", gtT_free)
            for h in range(H):
                em.do("tensor", "transpose",
                      out=PA[:, 0:1024].bitcast(BF16)[:, h * 128:(h + 1) * 128],
                      in_=gld2[:, h * 128:(h + 1) * 128],
                      identity=self.identt[:], sem=("SP" if h == H - 1 else None))
            trv = em.cnt["SP"]
            gld2_free = trv
            em.wait("vector", "SP", trv)
            em.wait("vector", "SP", gtT_free)
            em.do("vector", "tensor_copy", out=gtT[:],
                  in_=PA[:, 0:1024].bitcast(BF16)[:, 0:2048], sem="SV")
            gtv = em.cnt["SV"]
            pa_tr_free = gtv
            em.wait("tensor", "SV", gtv)
            em.wait("tensor", "SV", wov)
            em.wait("tensor", "SV", s_done.get(("ev", cp - 1), 0))
            for col in range(4):
                for h in range(H):
                    em.do("tensor", "matmul",
                          out=PB[:, col * 512:(col + 1) * 512],
                          lhsT=gtT[:, h * 128:(h + 1) * 128],
                          rhs=wosb[:, h * 2048 + col * 512:h * 2048 + (col + 1) * 512],
                          start=(h == 0), stop=(h == H - 1),
                          sem=("SP" if h == H - 1 else None))
            mmv = em.cnt["SP"]
            s_done[cp] = mmv
            gtT_free = mmv
            em.wait("vector", "SP", mmv)
            em.wait("vector", "SS", s_done.get(("sp", cp - 2), 0))
            em.do("vector", "tensor_copy", out=ssb[par][:], in_=PB[:], sem="SV")
            sev = em.cnt["SV"]
            s_done[("ev", cp)] = sev
            em.wait("sync", "SV", sev)
            s_done[("sp", cp)] = em.do("sync", "dma_start", out=self.scr_s[cp],
                                       in_=ssb[par][:], sem="SS", k=16)

    def c_ffn(self):
        em = self.em
        PA, PB = self.PA, self.PB
        sb4 = lambda n, s_, d: self.es4.enter_context(self.nc.sbuf_tensor(n, s_, d))
        NICB = 8
        NSUPC = 64 // NICB
        QT = 512
        NQ = OWN // QT
        NTC = QT // 128
        xns = sb4("xns", [128, 16 * 2048], BF16)
        acc = sb4("acc", [128, NTC * 2048], F32)
        fuT = sb4("fuT", [128, NICB * QT], BF16)
        wgb = [sb4(f"wgb{i}", [128, 16 * 128], BF16) for i in range(2)]
        wub = [sb4(f"wub{i}", [128, 16 * 128], BF16) for i in range(2)]
        wdall = sb4("wdall", [128, NICB * 2048], BF16)
        sgus = sb4("sgus", [128, 128], F32)
        sdns = sb4("sdns", [1, 2048], F32)
        sdnb = sb4("sdnb", [128, 2048], F32)
        fgat = sb4("fgat", [128, QT], F32)
        xtl = sb4("xtl", [128, 2048], F32)
        stl = sb4("stl", [128, 2048], BF16)
        otl = sb4("otl", [128, 2048], F32)
        em.do("sync", "dma_start", out=sgus[:], in_=self.sgu, sem="SX", k=16)
        em.do("sync", "dma_start", out=sdns[:], in_=self.sdn, sem="SX", k=16)
        em.do("sync", "dma_start",
              out=xns[:].rearrange("p (d t) -> p d t", d=16),
              in_=self.scr_xnT.rearrange("d p t -> p d t"), sem="SX", k=16)
        ldv = em.cnt["SX"]
        em.wait("tensor", "SX", ldv)
        for i in range(4):
            em.do("tensor", "matmul", out=PA[:, i * 512:(i + 1) * 512],
                  lhsT=self.onesrowf[:], rhs=sdns[0:1, i * 512:(i + 1) * 512],
                  start=True, stop=True, sem=("SP" if i == 3 else None))
        em.wait("vector", "SP")
        em.do("vector", "tensor_copy", out=sdnb[:], in_=PA[:], sem="SV")
        sdv = em.cnt["SV"]

        wld = {}
        wl_used = {}
        wl_next = [0]

        def load_gu(j):
            if j >= NQ * 64 or j < wl_next[0]:
                return
            wl_next[0] += 1
            ic = j % 64
            par = j % 2
            if j >= 2:
                em.wait("gpsimd", "SP", wl_used.get(j - 2, 0))
            em.do("gpsimd", "dma_start",
                  out=wgb[par][:].rearrange("p (d c) -> p d c", d=16),
                  in_=self.wg8[ic].rearrange("d p c -> p d c"), sem="SW", k=16)
            wld[j] = em.do("gpsimd", "dma_start",
                           out=wub[par][:].rearrange("p (d c) -> p d c", d=16),
                           in_=self.wu8[ic].rearrange("d p c -> p d c"),
                           sem="SW", k=16)

        load_gu(0)
        load_gu(1)
        acc_ev = {}
        out_done = {}
        stl_read = 0
        fu_free_sp = 0
        fgat_free = 0
        wd_free = 0
        gu_done = {0: (0, 0), 1: (0, 0)}
        for q in range(NQ):
            th0 = q * QT
            for p in range(NSUPC):
                em.wait("gpsimd", "SP", wd_free)
                for icb in range(NICB):
                    ic = p * NICB + icb
                    em.do("gpsimd", "dma_start",
                          out=wdall[:, icb * 2048:(icb + 1) * 2048],
                          in_=self.wd8[ic], sem="SW", k=16)
                wdv = em.cnt["SW"]
                for icb in range(NICB):
                    j = q * 64 + p * NICB + icb
                    ic = p * NICB + icb
                    par = j % 2
                    if j < 2:
                        em.wait("vector", "SW", wld[j])
                        em.do("vector", "tensor_copy", out=stl[:],
                              in_=wgb[par][:], sem="SV")
                        em.do("vector", "tensor_copy", out=stl[:],
                              in_=wub[par][:], sem="SV")
                        em.wait("tensor", "SV")
                    em.wait("tensor", "SW", wld[j])
                    em.wait("tensor", "SX", ldv)
                    em.wait("tensor", "SP", fu_free_sp)
                    em.wait("tensor", "SA", gu_done[icb % 2][0])
                    em.wait("tensor", "SV", gu_done[icb % 2][1])
                    gpar = icb % 2
                    for up in range(2):
                        wsb = wub[par] if up else wgb[par]
                        pslc = PB[:, gpar * 1024 + up * 512:
                                  gpar * 1024 + (up + 1) * 512]
                        for d in range(16):
                            em.do("tensor", "matmul", out=pslc,
                                  lhsT=wsb[:, d * 128:(d + 1) * 128],
                                  rhs=xns[:, d * 2048 + th0:d * 2048 + th0 + QT],
                                  start=(d == 0), stop=(d == 15),
                                  sem=("SP" if (d, up) == (15, 1) else None))
                    guv = em.cnt["SP"]
                    wl_used[j] = guv
                    em.wait("scalar", "SP", guv)
                    em.wait("scalar", "SV", fgat_free)
                    em.do("scalar", "activation", out=fgat[:],
                          in_=PB[:, gpar * 1024:gpar * 1024 + 512],
                          func=AF.Silu, scale=sgus[:, ic * 2:ic * 2 + 1],
                          sem="SA")
                    fgv = em.cnt["SA"]
                    em.wait("vector", "SA", fgv)
                    em.wait("vector", "SP", fu_free_sp)
                    em.do("vector", "scalar_tensor_tensor",
                          out=fuT[:, icb * QT:(icb + 1) * QT],
                          in0=PB[:, gpar * 1024 + 512:gpar * 1024 + 1024],
                          scalar=sgus[:, ic * 2 + 1:ic * 2 + 2],
                          in1=fgat[:], op0=ALU.mult, op1=ALU.mult, sem="SV")
                    fgat_free = em.cnt["SV"]
                    gu_done[gpar] = (em.cnt["SA"], em.cnt["SV"])
                    load_gu(j + 2)
                fuv = em.cnt["SV"]
                em.wait("tensor", "SV", fuv)
                em.wait("tensor", "SW", wdv)
                for tc in range(NTC):
                    prev = acc_ev.get((q, p, tc - 1),
                                      acc_ev.get((q, p - 1, NTC - 1),
                                                 acc_ev.get((q - 1, NSUPC - 1,
                                                             NTC - 1), sdv)))
                    em.wait("tensor", "SV", prev)
                    for col in range(4):
                        for icb in range(NICB):
                            em.do("tensor", "matmul",
                                  out=PA[:, col * 512:(col + 1) * 512],
                                  lhsT=fuT[:, icb * QT + tc * 128:
                                           icb * QT + (tc + 1) * 128],
                                  rhs=wdall[:, icb * 2048 + col * 512:
                                            icb * 2048 + (col + 1) * 512],
                                  start=(icb == 0), stop=(icb == NICB - 1),
                                  sem=("SP" if (icb, col) == (NICB - 1, 3)
                                       else None))
                    dnv = em.cnt["SP"]
                    em.wait("vector", "SP", dnv)
                    if p == 0:
                        em.wait("vector", "SA")
                        em.do("vector", "tensor_copy",
                              out=acc[:, tc * 2048:(tc + 1) * 2048], in_=PA[:],
                              sem="SV")
                    else:
                        em.do("vector", "tensor_add",
                              out=acc[:, tc * 2048:(tc + 1) * 2048],
                              in0=acc[:, tc * 2048:(tc + 1) * 2048], in1=PA[:],
                              sem="SV")
                    acc_ev[(q, p, tc)] = em.cnt["SV"]
                fu_free_sp = em.cnt["SP"]
                wd_free = em.cnt["SP"]
            for tc in range(NTC):
                gi = q * NTC + tc
                tok0 = 6144 + q * QT + tc * 128
                em.wait("sync", "SO", out_done.get(gi - 1, 0))
                xv = em.do("sync", "dma_start", out=xtl[:],
                           in_=self.xw[tok0:tok0 + 128, :], sem="SX", k=16)
                em.wait("scalar", "SV", stl_read)
                sv_ = em.do("scalar", "dma_start", out=stl[:],
                            in_=self.scr_s[gi], sem="SL", k=16)
                em.wait("vector", "SV", acc_ev[(q, NSUPC - 1, tc)])
                em.wait("vector", "SV", sdv)
                em.wait("vector", "SO", out_done.get(gi - 1, 0))
                em.vsync()
                em.do("vector", "tensor_mul",
                      out=acc[:, tc * 2048:(tc + 1) * 2048],
                      in0=acc[:, tc * 2048:(tc + 1) * 2048], in1=sdnb[:],
                      sem="SV")
                em.vsync()
                em.wait("scalar", "SV")
                em.wait("scalar", "SO", out_done.get(gi - 1, 0))
                em.do("scalar", "activation", out=otl[:],
                      in_=acc[:, tc * 2048:(tc + 1) * 2048], func=AF.Silu,
                      sem="SA")
                gv2 = em.cnt["SA"]
                em.wait("vector", "SA", gv2)
                em.wait("vector", "SL", sv_)
                em.do("vector", "tensor_mul", out=otl[:], in0=otl[:],
                      in1=stl[:], sem="SV")
                stl_read = em.cnt["SV"]
                em.vsync()
                em.wait("vector", "SX", xv)
                em.do("vector", "tensor_add", out=otl[:], in0=otl[:],
                      in1=xtl[:], sem="SV")
                em.vsync()
                ov = em.cnt["SV"]
                em.wait("sync", "SV", ov)
                out_done[gi] = em.do(
                    "sync", "dma_start",
                    out=self.out[q * QT + tc * 128:q * QT + (tc + 1) * 128, :],
                    in_=otl[:], sem="SO", k=16)


def _prep_shared(inputs):
    w = {}
    Wq = np.ascontiguousarray(inputs["L0_attn_W_qkv"], np.float32)
    kv = Wq[:, 2048:6144]
    w["wkv"] = np.ascontiguousarray(
        kv.reshape(16, 128, 8, 512).transpose(2, 0, 1, 3))
    qq = Wq[:, 0:2048]
    w["wqi"] = np.ascontiguousarray(
        qq.reshape(16, 128, 4, 512).transpose(2, 0, 1, 3))
    Wz = np.ascontiguousarray(inputs["L0_attn_W_z"], np.float32)
    w["wzi"] = np.ascontiguousarray(
        Wz.reshape(16, 128, 4, 512).transpose(2, 0, 1, 3))
    w["wb_in"] = np.ascontiguousarray(
        inputs["L0_attn_W_b"].astype(np.float32).reshape(16, 128, 16))
    w["wa_in"] = np.ascontiguousarray(
        inputs["L0_attn_W_a"].astype(np.float32).reshape(16, 128, 16))
    cw = np.asarray(inputs["L0_attn_conv_w"], np.float32)[:, 0, :]
    reorder = np.concatenate([cw[2048:4096], cw[4096:6144], cw[0:2048]], 0)
    w["cwi"] = np.ascontiguousarray(
        reorder.reshape(48, 128, 4).transpose(1, 0, 2).reshape(128, 192))
    w["adt"] = np.ascontiguousarray(
        np.stack([inputs["L0_attn_A_log"], inputs["L0_attn_dt_bias"]], 1)
        .astype(np.float32))
    w["nw_in"] = np.ascontiguousarray(
        np.broadcast_to(inputs["L0_attn_norm_w"].astype(np.float32)[None, :],
                        (128, 128)))
    w["wo"] = np.ascontiguousarray(
        inputs["L0_attn_W_out"].astype(np.float32).reshape(16, 128, 2048))
    w["wg8"] = np.ascontiguousarray(
        inputs["L0_ffn_w_gate"].reshape(16, 128, 64, 128).transpose(2, 0, 1, 3))
    w["wu8"] = np.ascontiguousarray(
        inputs["L0_ffn_w_up"].reshape(16, 128, 64, 128).transpose(2, 0, 1, 3))
    w["wd8"] = np.ascontiguousarray(
        inputs["L0_ffn_w_down"].reshape(64, 128, 2048))
    sg = inputs["L0_ffn_s_gate"].astype(np.float32).reshape(64, 128)
    su = inputs["L0_ffn_s_up"].astype(np.float32).reshape(64, 128)
    w["sgu"] = np.ascontiguousarray(
        np.stack([sg, su], -1).transpose(1, 0, 2).reshape(128, 128))
    w["sdn"] = np.ascontiguousarray(
        inputs["L0_ffn_s_down"].astype(np.float32)[None, :])
    w["identb"] = np.eye(128, dtype=BFD)
    idx = np.arange(128)
    low_s = idx[:, None] > idx[None, :]
    w["maskSi"] = np.where(idx[:, None] < idx[None, :], 0.0, NEGM
                           ).astype(np.float32)
    w["maskIi"] = np.where(idx[:, None] <= idx[None, :], 0.0, NEGM
                           ).astype(np.float32)
    return w


def _make_window(x, b, j):
    end = (j + 1) * OWN
    start = end - NT
    if start >= 0:
        return np.ascontiguousarray(x[b, start:end])
    pad = np.zeros((-start, D), np.float32)
    return np.ascontiguousarray(np.concatenate([pad, x[b, 0:end]], 0))


def kernel(**inputs):
    shared = _prep_shared(inputs)
    x = np.asarray(inputs["L0_attn_hidden_states"], np.float32)
    nc, _ = Builder(DEBUG_TAPS).build()
    in_maps = []
    for core in range(8):
        b, j = core // 4, core % 4
        m = dict(shared)
        m["xw"] = _make_window(x, b, j)
        in_maps.append(m)
    res = run_bass_kernel_spmd(nc, in_maps, core_ids=list(range(8)))
    outp = np.empty((B, S, D), np.float32)
    for core in range(8):
        b, j = core // 4, core % 4
        outp[b, j * OWN:(j + 1) * OWN] = res.results[core]["out"]
    return outp
